# version 22
# baseline (speedup 1.0000x reference)
import numpy as np
import concourse.bass as bass
import concourse.mybir as mybir
from concourse.bass_utils import run_bass_kernel_spmd

F32 = mybir.dt.float32
BF16 = mybir.dt.bfloat16
AF = mybir.ActivationFunctionType
ALU = mybir.AluOpType
AX = mybir.AxisListType

D = 4096
S = 4096
MY = 1024
NQ, NKV, DH = 32, 8, 128
NC = D // 128
QSCALE = DH ** -0.5
EPS = 1e-5
P0T = 256
NT0 = S // P0T
PRT = 256
NTP = MY // PRT
SB = 512


class Item:
    __slots__ = ("meth", "args", "kwargs", "incs")

    def __init__(self, meth, args, kwargs):
        self.meth, self.args, self.kwargs = meth, args, kwargs
        self.incs = []

    def then_inc(self, sem, k):
        self.incs.append((sem, k))
        return self


class WaitOn:
    __slots__ = ("names",)

    def __init__(self, names):
        self.names = names


class Stream:
    def __init__(self):
        self.items = []

    def __getattr__(self, meth):
        if meth.startswith("__"):
            raise AttributeError(meth)

        def f(*a, **kw):
            it = Item(meth, a, kw)
            self.items.append(it)
            return it
        return f

    def wait(self, *names):
        if names:
            self.items.append(WaitOn(names))


class Ctr:
    def __init__(self, sem, step=1):
        self.sem, self.step, self.n = sem, step, 0

    def tick(self, item):
        item.then_inc(self.sem, self.step)
        self.n += self.step
        return self.n


def build_nc():
    nc = bass.Bass("TRN2", target_bir_lowering=False)

    xTb = nc.dram_tensor("xTb", [D, S], F32, kind="ExternalInput").ap()
    xTmy = nc.dram_tensor("xTmy", [D, MY], F32, kind="ExternalInput").ap()
    wqT = nc.dram_tensor("wqT", [NQ, 128, NC, 128], F32, kind="ExternalInput").ap()
    wkT = nc.dram_tensor("wkT", [NKV, 128, NC, 128], F32, kind="ExternalInput").ap()
    wvR = nc.dram_tensor("wvR", [NC, 128, NKV * 128], F32, kind="ExternalInput").ap()
    woT = nc.dram_tensor("woT", [NC, 128, NC, 128], F32, kind="ExternalInput").ap()
    wupT = nc.dram_tensor("wupT", [NC, 128, NC, 128], F32, kind="ExternalInput").ap()
    gateT = nc.dram_tensor("gateT", [NC, 128, 4], F32, kind="ExternalInput").ap()
    ebias = nc.dram_tensor("ebias", [4, 1], F32, kind="ExternalInput").ap()
    aT = nc.dram_tensor("aT", [NC, 128, 32], F32, kind="ExternalInput").ap()
    bflat = nc.dram_tensor("bflat", [32, D], F32, kind="ExternalInput").ap()
    bvc = nc.dram_tensor("bvc", [128, NKV], F32, kind="ExternalInput").ap()
    outT = nc.dram_tensor("outT", [D, MY], F32, kind="ExternalOutput").ap()

    k_d = nc.dram_tensor("k_d", [NKV, 128, S], BF16, kind="Internal").ap()
    xnst = nc.dram_tensor("xnst", [128, NC, MY], BF16, kind="Internal").ap()
    v_d = nc.dram_tensor("v_d", [NKV, S // 128, 128, 128], BF16, kind="Internal").ap()

    from contextlib import ExitStack
    with ExitStack() as top:
        ones_col = top.enter_context(nc.sbuf_tensor("ones_col", [128, 1], BF16))
        ones_row = top.enter_context(nc.sbuf_tensor("ones_row", [1, 128], F32))
        cst_eps = top.enter_context(nc.sbuf_tensor("cst_eps", [1, 1], F32))
        eb_sb = top.enter_context(nc.sbuf_tensor("eb_sb", [4, 1], F32))
        bvc_sb = top.enter_context(nc.sbuf_tensor("bvc_sb", [128, NKV], F32))
        idn = top.enter_context(nc.sbuf_tensor("idn", [128, 128], F32))
        e4 = top.enter_context(nc.sbuf_tensor("e4", [4, 32], BF16))
        gate_sb = top.enter_context(nc.sbuf_tensor("gate_sb", [128, NC, 4], BF16))
        m_row = top.enter_context(nc.sbuf_tensor("m_row", [1, P0T], F32))
        v_row = top.enter_context(nc.sbuf_tensor("v_row", [1, P0T], F32))
        r_row = top.enter_context(nc.sbuf_tensor("r_row", [1, P0T], F32))

        pbp = [top.enter_context(nc.psum_tensor(f"pbp{i}", [128, 1024], F32))
               for i in range(4)]
        pb = [pbp[i // 2][:, (i % 2) * 512:(i % 2) * 512 + 512] for i in range(8)]

        s0 = ExitStack()
        x0 = [s0.enter_context(nc.sbuf_tensor(f"x0_{i}", [128, NC, P0T], BF16))
              for i in range(2)]
        wk_sb = s0.enter_context(nc.sbuf_tensor("wk_sb", [128, NKV, NC, 128], BF16))
        wv_sb = s0.enter_context(nc.sbuf_tensor("wv_sb", [128, NC, NKV * 128], BF16))
        xn0 = [s0.enter_context(nc.sbuf_tensor(f"xn0_{i}", [128, NC, P0T], BF16))
               for i in range(2)]
        x20 = s0.enter_context(nc.sbuf_tensor("x20", [128, 2, P0T], BF16))
        xf0 = s0.enter_context(nc.sbuf_tensor("xf0", [128, 2, P0T], F32))
        kt0 = [s0.enter_context(nc.sbuf_tensor(f"kt0_{i}", [128, P0T], BF16))
               for i in range(2)]
        vt0 = [s0.enter_context(nc.sbuf_tensor(f"vt0_{i}", [128, NKV * 128], BF16))
               for i in range(2)]
        s0.close()

        xn_my = top.enter_context(nc.sbuf_tensor("xn_my", [128, NC, MY], BF16))
        ctx_sb = top.enter_context(nc.sbuf_tensor("ctx_sb", [128, NC, MY], BF16))

        s1 = ExitStack()
        wq_sb = [s1.enter_context(nc.sbuf_tensor(f"wq_sb{i}", [128, NC, 128], BF16))
                 for i in range(2)]
        kh_sb = [s1.enter_context(nc.sbuf_tensor(f"kh_sb{i}", [128, S], BF16))
                 for i in range(2)]
        vh_sb = [s1.enter_context(nc.sbuf_tensor(f"vh_sb{i}", [128, S // 128, 128], BF16))
                 for i in range(2)]
        q_sb = s1.enter_context(nc.sbuf_tensor("q_sb", [128, MY], BF16))
        expp = s1.enter_context(nc.sbuf_tensor("expp", [128, 2, 2 * SB], BF16))
        dpair = s1.enter_context(nc.sbuf_tensor("dpair", [128, 10, SB], BF16))
        recip_b = s1.enter_context(nc.sbuf_tensor("recip_b", [128, SB], F32))
        ct_f = s1.enter_context(nc.sbuf_tensor("ct_f", [128, SB], F32))
        den_row = s1.enter_context(nc.sbuf_tensor("den_row", [1, SB], F32))
        s1.close()

        s2 = ExitStack()
        wo_sb = [s2.enter_context(nc.sbuf_tensor(f"wo_sb{i}", [128, NC, 128], BF16))
                 for i in range(2)]
        wu_sb = [s2.enter_context(nc.sbuf_tensor(f"wu_sb{i}", [128, NC, 128], BF16))
                 for i in range(2)]
        xres = [s2.enter_context(nc.sbuf_tensor(f"xres{i}", [128, MY], F32))
                for i in range(2)]
        at_sb = s2.enter_context(nc.sbuf_tensor("at_sb", [128, NC, 32], BF16))
        bf_sb = s2.enter_context(nc.sbuf_tensor("bf_sb", [32, D], BF16))
        lg_sb = s2.enter_context(nc.sbuf_tensor("lg_sb", [4, MY], F32))
        lgT_sb = s2.enter_context(nc.sbuf_tensor("lgT_sb", [128, MY // 128, 4], F32))
        mx_sb = s2.enter_context(nc.sbuf_tensor("mx_sb", [128, MY // 128, 1], F32))
        oh_sb = s2.enter_context(nc.sbuf_tensor("oh_sb", [128, MY // 128, 4], F32))
        ohT_sb = s2.enter_context(nc.sbuf_tensor("ohT_sb", [4, MY], BF16))
        m32_sb = s2.enter_context(nc.sbuf_tensor("m32_sb", [32, MY], BF16))
        zsel_sb = s2.enter_context(nc.sbuf_tensor("zsel_sb", [32, MY], BF16))
        g_sb = s2.enter_context(nc.sbuf_tensor("g_sb", [128, SB], F32))
        t_sb = s2.enter_context(nc.sbuf_tensor("t_sb", [128, SB], F32))
        ot_sb = [s2.enter_context(nc.sbuf_tensor(f"ot_sb{i}", [128, MY], F32))
                 for i in range(2)]
        s2.close()

        sPE = Ctr(top.enter_context(nc.semaphore("sPE")))
        sDVE = Ctr(top.enter_context(nc.semaphore("sDVE")))
        sACT = Ctr(top.enter_context(nc.semaphore("sACT")))
        sPOOL = Ctr(top.enter_context(nc.semaphore("sPOOL")))
        def dsem(name):
            return Ctr(top.enter_context(nc.semaphore(name)), 16)

        qW0 = dsem("qW0")
        qX = dsem("qX")
        qK = [dsem("qK0"), dsem("qK1")]
        qV = [dsem("qV0"), dsem("qV1")]
        qQ = [dsem("qQ0"), dsem("qQ1")]
        qWo = [dsem("qWo0"), dsem("qWo1")]
        qWu = [dsem("qWu0"), dsem("qWu1")]
        qWb = dsem("qWb")
        qR = [dsem("qR0"), dsem("qR1")]
        qSK = [dsem("qSK0"), dsem("qSK1")]
        qSV = [dsem("qSV0"), dsem("qSV1")]
        qSO = [dsem("qSO0"), dsem("qSO1")]
        qXS = [dsem("qXS0"), dsem("qXS1")]
        qXM = dsem("qXM")
        qWK = [dsem(f"qWK{h}") for h in range(NKV)]
        qWV = [dsem("qWVa"), dsem("qWVb")]
        all_q = ([qW0, qX, qWb, qXM] + qK + qV + qQ + qWo + qWu + qR
                 + qSK + qSV + qSO + qXS + qWK + qWV)

        K = {}

        def mark(name, ctr, item):
            K[name] = (ctr.sem, ctr.tick(item))

        T, V, A, P, Y = Stream(), Stream(), Stream(), Stream(), Stream()

        sPOOL.tick(P.memset(ones_col[:], 1.0))
        sPOOL.tick(P.memset(ones_row[:], 1.0))
        sPOOL.tick(P.memset(cst_eps[:], EPS))
        mark("idn0", sPOOL, P.memset(idn[:], 0.0))
        P.wait("idn0")
        mark("idn", sPOOL,
             P.affine_select(out=idn[:], in_=idn[:], compare_op=ALU.not_equal,
                             fill=1.0, base=0, pattern=[[-1, 128]],
                             channel_multiplier=1))
        mark("e40", sPOOL, P.memset(e4[:], 1.0))
        P.wait("e40")
        mark("e41", sPOOL, P.affine_select(
            e4[:], e4[:], pattern=[[1, 32]], channel_multiplier=-8,
            base=0, compare_op=ALU.is_ge, fill=0.0))
        P.wait("e41")
        mark("consts", sPOOL, P.affine_select(
            e4[:], e4[:], pattern=[[-1, 32]], channel_multiplier=8,
            base=7, compare_op=ALU.is_ge, fill=0.0))
        K["e4done"] = K["consts"]
        qW0.tick(P.dma_start(eb_sb[:], ebias))
        qW0.tick(P.dma_start(bvc_sb[:], bvc))
        mark("gws", qW0, P.dma_start(gate_sb[:], gateT.rearrange("c p e -> p c e")))

        chunk_par = [0]

        def ln_tile(tag, xsrc, xn_dst, n_tok, first_waits, prev=None,
                    norm_first_waits=(), bcp=(2, 3)):
            ps_sum = pb[0][0:1, 0:n_tok]
            ps_sq = pb[1][0:1, 0:n_tok]
            ps_mb = pb[bcp[0]][:, 0:n_tok]
            ps_rb = pb[bcp[1]][:, 0:n_tok]
            for c in range(NC):
                j = chunk_par[0]
                chunk_par[0] += 1
                sl = j % 2
                V.wait(f"{tag}src" if c == 0 else None) if c == 0 else None
                if c == 0:
                    V.wait(f"{tag}src")
                if j >= 2:
                    V.wait(f"lnB_{j-2}")
                mark(f"lnq_{j}", sDVE,
                     V.tensor_mul(x20[:, sl, 0:n_tok], xsrc[:, c, 0:n_tok],
                                  xsrc[:, c, 0:n_tok]))
                T.wait(f"lnq_{j}")
                if c == 0:
                    T.wait(f"{tag}src", *first_waits)
                T.matmul(ps_sum, ones_col[:], xsrc[:, c, 0:n_tok],
                         start=(c == 0), stop=(c == NC - 1))
                i2 = T.matmul(ps_sq, ones_col[:], x20[:, sl, 0:n_tok],
                              start=(c == 0), stop=(c == NC - 1))
                mark(f"lnB_{j}", sPE, i2)
            jl = chunk_par[0] - 1
            V.wait(f"lnB_{jl}")
            if prev is not None:
                V.wait(f"{prev}bm", f"{prev}bc", f"{prev}ln")
            mark(f"{tag}m", sDVE,
                 V.tensor_scalar_mul(m_row[:, 0:n_tok], ps_sum, 1.0 / D))
            mark(f"{tag}v0", sDVE,
                 V.tensor_scalar_mul(v_row[:, 0:n_tok], ps_sq, 1.0 / D))
            V.wait(f"{tag}m", f"{tag}v0")
            mark(f"{tag}m2", sDVE,
                 V.tensor_mul(r_row[:, 0:n_tok], m_row[:, 0:n_tok],
                              m_row[:, 0:n_tok]))
            V.wait(f"{tag}m2")
            mark(f"{tag}var", sDVE,
                 V.tensor_sub(v_row[:, 0:n_tok], v_row[:, 0:n_tok],
                              r_row[:, 0:n_tok]))
            A.wait(f"{tag}var")
            if prev is not None:
                A.wait(f"{prev}bc")
            mark(f"{tag}ln", sACT,
                 A.activation(r_row[:, 0:n_tok], v_row[:, 0:n_tok], AF.Ln,
                              bias=cst_eps[:, :]))
            A.wait(f"{tag}ln")
            mark(f"{tag}rstd", sACT,
                 A.activation(r_row[:, 0:n_tok], r_row[:, 0:n_tok], AF.Exp,
                              scale=-0.5))
            T.wait(f"{tag}m")
            mark(f"{tag}bm", sPE,
                 T.matmul(ps_mb, ones_row[:], m_row[:, 0:n_tok],
                          start=True, stop=True))
            T.wait(f"{tag}rstd")
            mark(f"{tag}bc", sPE,
                 T.matmul(ps_rb, ones_row[:], r_row[:, 0:n_tok],
                          start=True, stop=True))
            for c in range(NC):
                sl = c % 2
                V.wait(f"{tag}bc")
                if c == 0:
                    V.wait(*norm_first_waits)
                mark(f"{tag}sub{c}", sDVE,
                     V.tensor_sub(xf0[:, sl, 0:n_tok], xsrc[:, c, 0:n_tok], ps_mb))
                V.wait(f"{tag}sub{c}")
                mark(f"{tag}xn{c}", sDVE,
                     V.tensor_tensor(xn_dst(c), xf0[:, sl, 0:n_tok], ps_rb,
                                     ALU.mult))


        def kv_emit(tt, xtag):
            for h in range(NKV):
                hh = tt * NKV + h
                if tt == 0:
                    T.wait(f"wkh_{h}")
                if hh > 0:
                    T.wait(f"kev_{hh-1}")
                if h == 0 and tt > 0:
                    T.wait(f"vev_{tt*2-1}_1")
                for c in range(NC):
                    if h == 0:
                        T.wait(f"{xtag}xn{c}")
                    i = T.matmul(pb[4][:, 0:P0T], wk_sb[:, h, c, :],
                                 xn0[tt % 2][:, c, :],
                                 start=(c == 0), stop=(c == NC - 1))
                mark(f"kmm_{hh}", sPE, i)
                A.wait(f"kmm_{hh}")
                if hh >= 2:
                    A.wait(f"kst_{hh-2}")
                mark(f"kev_{hh}", sACT,
                     A.activation(kt0[h % 2][:], pb[4][:, 0:P0T], AF.Copy))
                A.wait(f"kev_{hh}")
                mark(f"kst_{hh}", qSK[hh % 2],
                     A.dma_start(k_d[h, :, tt * P0T:(tt + 1) * P0T], kt0[h % 2][:]))
            for tc in range(P0T // 128):
                tcg = tt * (P0T // 128) + tc
                if tcg > 0:
                    T.wait(f"vev_{tcg-1}_0", f"vev_{tcg-1}_1")
                T.wait(f"kev_{tt*NKV+NKV-1}")
                for half in range(2):
                    ps = pb[5 if half == 0 else 4][:, 0:512]
                    if tt == 0 and tc == 0:
                        T.wait("wva" if half == 0 else "w0")
                    for c in range(NC):
                        i = T.matmul(ps, xn0[tt % 2][:, c, tc * 128:(tc + 1) * 128],
                                     wv_sb[:, c, half * 512:(half + 1) * 512],
                                     start=(c == 0), stop=(c == NC - 1))
                    mark(f"vmm_{tcg}_{half}", sPE, i)
                for half in range(2):
                    A.wait(f"vmm_{tcg}_{half}")
                    if tcg >= 2:
                        A.wait(f"vst_{tcg-2}")
                    mark(f"vev_{tcg}_{half}", sACT,
                         A.activation(vt0[tc % 2][:, half * 512:(half + 1) * 512],
                                      pb[5 if half == 0 else 4][:, 0:512], AF.Copy))
                A.wait(f"vev_{tcg}_0", f"vev_{tcg}_1")
                for h in range(NKV):
                    i = A.dma_start(v_d[h, tcg], vt0[tc % 2][:, h * 128:(h + 1) * 128])
                    if h < NKV - 1:
                        qSV[tcg % 2].tick(i)
                mark(f"vst_{tcg}", qSV[tcg % 2], i)

        seq = []
        bi = 0
        for i4 in range(4):
            seq += [("b", bi), ("b", bi + 1), ("b", bi + 2), ("b", bi + 3)]
            seq.append(("p", i4))
            bi += 4
        kvq = []
        done_kv = 0
        for sidx, (kind, ti) in enumerate(seq):
            tg = f"t{sidx}"
            if sidx >= 2:
                P.wait(f"t{sidx-2}xn{NC-1}", f"lnB_{(sidx-1)*NC-1}")
                if seq[sidx - 2][0] == "p":
                    P.wait(f"xst_{seq[sidx-2][1]}")
            if sidx >= 1:
                P.wait(f"lnq_{(sidx-1)*NC}")
            if kind == "b":
                src_ap = xTb.rearrange("(c p) t -> p c t", p=128)[
                    :, :, ti * P0T:(ti + 1) * P0T]
            else:
                src_ap = xTmy.rearrange("(c p) t -> p c t", p=128)[
                    :, :, ti * P0T:(ti + 1) * P0T]
            mark(f"{tg}src", qX, P.dma_start(x0[sidx % 2][:], src_ap))
            if sidx == 1:
                for h in range(2):
                    mark(f"wkh_{h}", qWK[h],
                         P.dma_start(wk_sb[:, h], wkT[h]))
            elif sidx == 2:
                mark("wva", qWV[0],
                     P.dma_start(wv_sb[:, :, 0:512],
                                 wvR.rearrange("c p v -> p c v")[:, :, 0:512]))
                for h in range(2, 4):
                    mark(f"wkh_{h}", qWK[h],
                         P.dma_start(wk_sb[:, h], wkT[h]))
            elif sidx == 3:
                mark("w0", qWV[1],
                     P.dma_start(wv_sb[:, :, 512:1024],
                                 wvR.rearrange("c p v -> p c v")[:, :, 512:1024]))
                for h in range(4, NKV):
                    mark(f"wkh_{h}", qWK[h],
                         P.dma_start(wk_sb[:, h], wkT[h]))
            fw = ["consts"] if sidx == 0 else [f"t{sidx-1}m", f"t{sidx-1}v0"]
            if sidx >= 2:
                fw.append(f"t{sidx-2}xn{NC-1}")
            if kind == "b":
                nfw = [f"vmm_{(ti-2)*2+1}_1"] if ti >= 2 else ()
                xnd = (lambda c, sl=ti % 2: xn0[sl][:, c, 0:P0T])
            else:
                nfw = ()
                xnd = (lambda c, sl=sidx % 2: x0[sl][:, c, 0:P0T])
            ln_tile(tg, x0[sidx % 2], xnd, P0T, fw,
                    prev=(f"t{sidx-1}" if sidx > 0 else None),
                    norm_first_waits=nfw,
                    bcp=((2, 3) if sidx % 2 == 0 else (6, 7)))
            if kind == "b":
                kvq.append((ti, tg, sidx))
            else:
                Y.wait(f"{tg}xn{NC-1}")
                mark(f"xst_{ti}", qXS[ti % 2],
                     Y.dma_start(xnst[:, :, ti * P0T:(ti + 1) * P0T],
                                 x0[sidx % 2][:]))
            if kvq and kvq[0][2] < sidx:
                bti, btg, _ = kvq.pop(0)
                kv_emit(bti, btg)
        while kvq:
            bti, btg, _ = kvq.pop(0)
            kv_emit(bti, btg)

        k_done = (f"kst_{NT0*NKV-2}", f"kst_{NT0*NKV-1}")
        v_done = (f"vst_{NT0*(P0T//128)-2}", f"vst_{NT0*(P0T//128)-1}")
        p0_pe_done = f"vmm_{NT0*(P0T//128)-1}_1"

        Y.wait("xst_0", "xst_1", "xst_2", "xst_3",
               f"t{len(seq)-1}xn{NC-1}", f"lnB_{len(seq)*NC-1}",
               *k_done, *v_done, p0_pe_done)
        mark("xnmy", qXM, Y.dma_start(xn_my[:], xnst))

        pro_done = "xnmy"
        pro_rows = [f"t{NT0+NTP-1}m", f"t{NT0+NTP-1}v0"]

        for h in range(NKV):
            P.wait(*k_done, *v_done)
            if h >= 2:
                P.wait(f"qk_{4*(h-2)+3}_1_31")
            mark(f"kh_{h}", qK[h % 2], P.dma_start(kh_sb[h % 2][:], k_d[h]))
            if h >= 2:
                P.wait(f"pvl_{4*(h-2)+3}_1")
            mark(f"vh_{h}", qV[h % 2],
                 P.dma_start(vh_sb[h % 2][:],
                             v_d[h].rearrange("c p d -> p c d")))
            for g in range(4):
                qh = h * 4 + g
                if qh < 2:
                    P.wait(pro_done, *pro_rows, *k_done, *v_done, p0_pe_done)
                else:
                    P.wait(f"qmm_{qh-2}_1")
                mark(f"wq_{qh}", qQ[qh % 2], P.dma_start(wq_sb[qh % 2][:], wqT[qh]))

        gchunk = [0]

        def qproj_emit(qh, interleave=None):
            if interleave is None:
                T.wait(f"wq_{qh}")
                if qh == 0:
                    T.wait(pro_done, *pro_rows)
                for sb2 in range(2):
                    for c in range(NC):
                        i = T.matmul(pb[sb2][:, 0:512], wq_sb[qh % 2][:, c, :],
                                     xn_my[:, c, sb2 * 512:(sb2 + 1) * 512],
                                     start=(c == 0), stop=(c == NC - 1))
                    mark(f"qmm_{qh}_{sb2}", sPE, i)
                return
            it = interleave
            sb2, c = it // NC, it % NC
            if it == 0:
                T.wait(f"wq_{qh}")
                T.wait(f"qev_{qh-1}_0", f"qev_{qh-1}_1")
            i = T.matmul(pb[sb2][:, 0:512], wq_sb[qh % 2][:, c, :],
                         xn_my[:, c, sb2 * 512:(sb2 + 1) * 512],
                         start=(c == 0), stop=(c == NC - 1))
            if c == NC - 1:
                mark(f"qmm_{qh}_{sb2}", sPE, i)

        def qev_emit(qh):
            for sb2 in range(2):
                A.wait(f"qmm_{qh}_{sb2}")
                if qh > 0:
                    A.wait(f"qk_{qh-1}_1_31")
                mark(f"qev_{qh}_{sb2}", sACT,
                     A.activation(q_sb[:, sb2 * 512:(sb2 + 1) * 512],
                                  pb[sb2][:, 0:512], AF.Copy, scale=QSCALE))

        qproj_emit(0)
        qev_emit(0)
        pending_bc = [None]
        pending_vt = [None]
        for kvh in range(NKV):
            for g in range(4):
                qh = kvh * 4 + g
                for sb in range(2):
                    jbase = gchunk[0]
                    pbase = jbase // 2
                    T.wait(f"qev_{qh}_{sb}", f"kh_{kvh}")
                    SCP = (pbp[1], pbp[3])
                    for c in range(4):
                        j = jbase + c
                        p = j // 2
                        if p >= 2 and c % 2 == 0:
                            T.wait(f"exp_{p-2}")
                        i = T.matmul(SCP[p % 2][:, (c % 2) * 512:(c % 2) * 512 + 512],
                                     kh_sb[kvh % 2][:, c * 128:(c + 1) * 128],
                                     q_sb[:, sb * 512:(sb + 1) * 512],
                                     start=True, stop=True)
                        if c % 2 == 1:
                            mark(f"qk_{qh}_{sb}_{c}", sPE, i)
                    for c in range(NC):
                        j = jbase + c
                        p = j // 2
                        if c % 2 == 1:
                            A.wait(f"qk_{qh}_{sb}_{c}")
                            if p >= 2:
                                A.wait(f"pr_{p-2}")
                            mark(f"exp_{p}", sACT,
                                 A.activation(expp[:, p % 2, :],
                                              SCP[p % 2][:, 0:1024], AF.Exp))
                        T.wait(f"exp_{p}")
                        if c == 0:
                            T.wait(f"vh_{kvh}")
                            if qh > 0 or sb > 0:
                                pq, psb = (qh, 0) if sb else (qh - 1, 1)
                                T.wait(f"ctxm_{pq}_{psb}")
                        i = T.matmul(pb[4][:, 0:SB], vh_sb[kvh % 2][:, c, :],
                                     expp[:, p % 2, (c % 2) * 512:(c % 2) * 512 + 512],
                                     start=(c == 0), stop=(c == NC - 1))
                        if c == NC - 1:
                            mark(f"pvl_{qh}_{sb}", sPE, i)
                        if qh < NQ - 1:
                            qproj_emit(qh + 1, interleave=sb * NC + c)
                        if c % 2 == 1:
                            V.wait(f"exp_{p}")
                            if qh == 0 and sb == 0 and c == 1:
                                V.wait(f"vev_{NT0*(P0T//128)-1}_1")
                            if p >= 2:
                                V.wait(f"qd_{(j-4)//4}")
                            mark(f"pr_{p}", sDVE,
                                 V.tensor_add(dpair[:, 8 + p % 2, :],
                                              expp[:, p % 2, 0:512],
                                              expp[:, p % 2, 512:1024]))
                        if c % 4 == 3:
                            V.wait(f"pr_{j//2}", f"pr_{(j-2)//2}")
                            if c == 3 and (qh > 0 or sb > 0):
                                V.wait(f"den_{jbase-1}")
                            mark(f"qd_{j//4}", sDVE,
                                 V.tensor_add(dpair[:, (c - 3) // 4, :],
                                              dpair[:, 8, :],
                                              dpair[:, 9, :]))
                        if c == 5 and pending_vt[0] is not None:
                            V.wait("gws")
                            pq, psb, pkv = pending_vt[0]
                            V.wait(f"bc_{pq}_{psb}")
                            mark(f"rc_{pq}_{psb}", sDVE,
                                 V.tensor_copy(recip_b[:], pb[5][:, 0:SB]))
                            V.wait(f"rc_{pq}_{psb}")
                            mark(f"ctxn_{pq}_{psb}", sDVE,
                                 V.tensor_tensor(ct_f[:], ct_f[:], recip_b[:],
                                                 ALU.mult))
                            V.wait(f"ctxn_{pq}_{psb}")
                            mark(f"ctxw_{pq}_{psb}", sDVE,
                                 V.tensor_scalar(
                                     ctx_sb[:, pq, psb * 512:(psb + 1) * 512],
                                     ct_f[:], bvc_sb[:, pkv:pkv + 1], None,
                                     op0=ALU.add))
                            pending_vt[0] = None
                        if c == 3 and pending_bc[0] is not None:
                            pq, psb = pending_bc[0]
                            T.wait(f"dre_{pq}_{psb}")
                            T.wait(f"denr_{pq}_{psb}")
                            if pq > 0 or psb > 0:
                                p2q, p2sb = (pq, 0) if psb else (pq - 1, 1)
                                T.wait(f"rc_{p2q}_{p2sb}")
                            mark(f"bc_{pq}_{psb}", sPE,
                                 T.matmul(pb[5][:, 0:SB], ones_row[:], den_row[:],
                                          start=True, stop=True))
                            pending_bc[0] = None
                        if c % 2 == 1 and c + 4 < NC:
                            for cc in (c + 3, c + 4):
                                j2 = jbase + cc
                                p2 = j2 // 2
                                if cc % 2 == 0:
                                    T.wait(f"exp_{p2-2}")
                                i = T.matmul(
                                    SCP[p2 % 2][:, (cc % 2) * 512:(cc % 2) * 512 + 512],
                                    kh_sb[kvh % 2][:, cc * 128:(cc + 1) * 128],
                                    q_sb[:, sb * 512:(sb + 1) * 512],
                                    start=True, stop=True)
                                if cc % 2 == 1:
                                    mark(f"qk_{qh}_{sb}_{cc}", sPE, i)
                    if qh > 0 or sb > 0:
                        pq, psb = (qh, 0) if sb else (qh - 1, 1)
                        T.wait(f"rc_{pq}_{psb}")
                    for i2 in range(NC // 4):
                        T.wait(f"qd_{(jbase + 4*i2 + 3)//4}")
                        di = T.matmul(pb[5][0:1, 0:SB], ones_col[:],
                                      dpair[:, i2, :],
                                      start=(i2 == 0), stop=(i2 == NC // 4 - 1))
                    mark(f"den_{jbase+NC-1}", sPE, di)
                    gchunk[0] += NC
                    V.wait(f"pvl_{qh}_{sb}")
                    if qh > 0 or sb > 0:
                        pq, psb = (qh, 0) if sb else (qh - 1, 1)
                        V.wait(f"ctxw_{pq}_{psb}")
                    mark(f"ctxm_{qh}_{sb}", sDVE,
                         V.tensor_copy(ct_f[:], pb[4][:, 0:SB]))
                    V.wait(f"den_{jbase+NC-1}")
                    if qh > 0 or sb > 0:
                        pq, psb = (qh, 0) if sb else (qh - 1, 1)
                        V.wait(f"bc_{pq}_{psb}")
                        V.wait(f"dre_{pq}_{psb}")
                    mark(f"denr_{qh}_{sb}", sDVE,
                         V.tensor_copy(den_row[:], pb[5][0:1, 0:SB]))
                    A.wait(f"denr_{qh}_{sb}")
                    mark(f"dln_{qh}_{sb}", sACT,
                         A.activation(den_row[:], den_row[:], AF.Ln))
                    A.wait(f"dln_{qh}_{sb}")
                    mark(f"dre_{qh}_{sb}", sACT,
                         A.activation(den_row[:], den_row[:], AF.Exp, scale=-1.0))
                    pending_bc[0] = (qh, sb)
                    pending_vt[0] = (qh, sb, kvh)
                if qh < NQ - 1:
                    qev_emit(qh + 1)

        pq, psb = pending_bc[0]
        T.wait(f"dre_{pq}_{psb}", f"denr_{pq}_{psb}")
        p2q, p2sb = (pq, 0) if psb else (pq - 1, 1)
        T.wait(f"rc_{p2q}_{p2sb}")
        mark(f"bc_{pq}_{psb}", sPE,
             T.matmul(pb[5][:, 0:SB], ones_row[:], den_row[:],
                      start=True, stop=True))
        pending_bc[0] = None
        pq, psb, pkv = pending_vt[0]
        V.wait("gws")
        V.wait(f"bc_{pq}_{psb}")
        mark(f"rc_{pq}_{psb}", sDVE, V.tensor_copy(recip_b[:], pb[5][:, 0:SB]))
        V.wait(f"rc_{pq}_{psb}")
        mark(f"ctxn_{pq}_{psb}", sDVE,
             V.tensor_tensor(ct_f[:], ct_f[:], recip_b[:], ALU.mult))
        V.wait(f"ctxn_{pq}_{psb}")
        mark(f"ctxw_{pq}_{psb}", sDVE,
             V.tensor_scalar(ctx_sb[:, pq, psb * 512:(psb + 1) * 512],
                             ct_f[:], bvc_sb[:, pkv:pkv + 1], None, op0=ALU.add))
        pending_vt[0] = None

        att_done = f"ctxw_{NQ-1}_1"
        att_pe_done = f"den_{gchunk[0]-1}"

        P.wait(att_done, att_pe_done)
        qWb.tick(P.dma_start(at_sb[:], aT.rearrange("c p e -> p c e")))
        mark("w1b", qWb, P.dma_start(bf_sb[:], bflat))
        for oc in range(NC):
            if oc >= 2:
                P.wait(f"womm_{oc-2}_1")
            mark(f"wo_{oc}", qWo[oc % 2], P.dma_start(wo_sb[oc % 2][:], woT[oc]))
            if oc >= 2:
                P.wait(f"ffnm_{oc-2}_1")
            mark(f"wu_{oc}", qWu[oc % 2], P.dma_start(wu_sb[oc % 2][:], wupT[oc]))
            if oc < 2:
                Y.wait(att_done)
            else:
                Y.wait(f"ow_{oc-2}_1")
            mark(f"xr_{oc}", qR[oc % 2],
                 Y.dma_start(xres[oc % 2][:], xTmy[oc * 128:(oc + 1) * 128, :]))

        def womm_emit(oc):
            T.wait(f"wo_{oc}")
            if oc >= 2:
                T.wait(f"tm_{oc-2}_0", f"tm_{oc-2}_1")
            pp = 4 if oc % 2 == 0 else 6
            for sb in range(2):
                for c in range(NC):
                    i = T.matmul(pb[pp + sb][:, 0:512], wo_sb[oc % 2][:, c, :],
                                 ctx_sb[:, c, sb * 512:(sb + 1) * 512],
                                 start=(c == 0), stop=(c == NC - 1))
                mark(f"womm_{oc}_{sb}", sPE, i)

        def ffnm_emit(oc):
            T.wait(f"wu_{oc}", "zs_0", "zs_1")
            if oc >= 2:
                T.wait(f"gev_{oc-2}_1")
            elif oc == 0:
                T.wait("lgev_0", "lgev_1")
            qp = 0 if oc % 2 == 0 else 2
            for sb in range(2):
                for c in range(NC):
                    T.matmul(pb[qp + sb][:, 0:512], wu_sb[oc % 2][:, c, :],
                             xn_my[:, c, sb * 512:(sb + 1) * 512],
                             start=(c == 0), stop=False)
                mark(f"ffnm_{oc}_{sb}", sPE,
                     T.matmul(pb[qp + sb][:, 0:512],
                              bf_sb[:, oc * 128:(oc + 1) * 128],
                              zsel_sb[:, sb * 512:(sb + 1) * 512],
                              start=False, stop=True))

        def tail_emit(oc):
            pp = 4 if oc % 2 == 0 else 6
            qp = 0 if oc % 2 == 0 else 2
            for sb in range(2):
                A.wait(f"ffnm_{oc}_{sb}")
                if oc > 0 or sb > 0:
                    poc, psb = (oc, 0) if sb else (oc - 1, 1)
                    A.wait(f"tm_{poc}_{psb}")
                mark(f"gev_{oc}_{sb}", sACT,
                     A.activation(g_sb[:], pb[qp + sb][:, 0:512], AF.Silu))
                V.wait(f"womm_{oc}_{sb}", f"gev_{oc}_{sb}")
                if oc > 0 or sb > 0:
                    poc, psb = (oc, 0) if sb else (oc - 1, 1)
                    V.wait(f"ow_{poc}_{psb}")
                mark(f"tm_{oc}_{sb}", sDVE,
                     V.tensor_tensor(t_sb[:], pb[pp + sb][:, 0:512],
                                     g_sb[:], ALU.mult))
                V.wait(f"tm_{oc}_{sb}", f"xr_{oc}")
                if oc >= 2:
                    V.wait(f"ost_{oc-2}")
                mark(f"ow_{oc}_{sb}", sDVE,
                     V.tensor_add(ot_sb[oc % 2][:, sb * 512:(sb + 1) * 512],
                                  t_sb[:],
                                  xres[oc % 2][:, sb * 512:(sb + 1) * 512]))
            A.wait(f"ow_{oc}_0", f"ow_{oc}_1")
            mark(f"ost_{oc}", qSO[oc % 2],
                 A.dma_start(outT[oc * 128:(oc + 1) * 128, :], ot_sb[oc % 2][:]))

        T.wait(att_done, att_pe_done, "gws")
        for sb in range(2):
            T.wait(f"qev_{NQ-1}_{sb}")
            for c in range(NC):
                i = T.matmul(pb[sb][0:4, 0:512], gate_sb[:, c, :],
                             xn_my[:, c, sb * 512:(sb + 1) * 512],
                             start=(c == 0), stop=(c == NC - 1))
            mark(f"lgmm_{sb}", sPE, i)
            V.wait(f"lgmm_{sb}", att_done, "gws")
            mark(f"lgev_{sb}", sDVE,
                 V.tensor_scalar(lg_sb[:, sb * 512:(sb + 1) * 512],
                                 pb[sb][0:4, 0:512], eb_sb[:, 0:1], None,
                                 op0=ALU.add))
        womm_emit(0)
        womm_emit(1)
        for b in range(MY // 128):
            T.wait(f"lgev_{b // 4}", "idn")
            if b > 0:
                T.wait(f"lgtc_{b-1}")
            mark(f"lgt_{b}", sPE,
                 T.transpose(pb[2][:, 0:4], lg_sb[:, b * 128:(b + 1) * 128],
                             idn[0:4, 0:4]))
            V.wait(f"lgt_{b}")
            mark(f"lgtc_{b}", sDVE, V.tensor_copy(lgT_sb[:, b, :], pb[2][:, 0:4]))
        for b in range(MY // 128):
            V.wait(f"lgtc_{b}")
            mark(f"mx_{b}", sDVE,
                 V.reduce_max(mx_sb[:, b, :], lgT_sb[:, b, :], axis=AX.X))
            V.wait(f"mx_{b}")
            mark(f"oh_{b}", sDVE,
                 V.tensor_scalar(oh_sb[:, b, :], lgT_sb[:, b, :], mx_sb[:, b, 0:1],
                                 None, op0=ALU.is_ge))
        for b in range(MY // 128):
            T.wait(f"oh_{b}", f"lgtc_{MY//128-1}")
            if b > 0:
                T.wait(f"ohtc_{b-1}")
            mark(f"oht_{b}", sPE,
                 T.transpose(pb[2][0:4, 0:128], oh_sb[:, b, :], idn[:, :]))
            V.wait(f"oht_{b}")
            mark(f"ohtc_{b}", sDVE,
                 V.tensor_copy(ohT_sb[:, b * 128:(b + 1) * 128], pb[2][0:4, 0:128]))
        for sb in range(2):
            T.wait(f"ohtc_{MY//128-1}", "e4done")
            if sb > 0:
                T.wait("m32e_0")
            mark(f"m32m_{sb}", sPE,
                 T.matmul(pb[2][0:32, 0:512], e4[:],
                          ohT_sb[:, sb * 512:(sb + 1) * 512], start=True, stop=True))
            V.wait(f"m32m_{sb}")
            mark(f"m32e_{sb}", sDVE,
                 V.tensor_copy(m32_sb[:, sb * 512:(sb + 1) * 512],
                               pb[2][0:32, 0:512]))
        for sb in range(2):
            if sb > 0:
                T.wait("zs_0")
            else:
                T.wait("w1b")
            for c in range(NC):
                i = T.matmul(pb[3][0:32, 0:512], at_sb[:, c, :],
                             xn_my[:, c, sb * 512:(sb + 1) * 512],
                             start=(c == 0), stop=(c == NC - 1))
            mark(f"zmm_{sb}", sPE, i)
            V.wait(f"zmm_{sb}", f"m32e_{sb}")
            mark(f"zs_{sb}", sDVE,
                 V.tensor_tensor(zsel_sb[:, sb * 512:(sb + 1) * 512],
                                 pb[3][0:32, 0:512],
                                 m32_sb[:, sb * 512:(sb + 1) * 512], ALU.mult))
        ffnm_emit(0)
        tail_emit(0)
        ffnm_emit(1)
        tail_emit(1)
        for oc in range(2, NC):
            womm_emit(oc)
            ffnm_emit(oc)
            tail_emit(oc)

        Y.wait(f"ost_{NC-2}", f"ost_{NC-1}")
        Y.items.append(WaitOn(tuple(f"__q{i}__" for i in range(len(all_q)))))

        for i, q in enumerate(all_q):
            K[f"__q{i}__"] = (q.sem, q.n)

        def replay(items):
            def f(eng):
                seen = {}
                for it in items:
                    if isinstance(it, WaitOn):
                        for nm in it.names:
                            if nm is None:
                                continue
                            sem, n = K[nm]
                            if seen.get(sem.num, -1) >= n:
                                continue
                            seen[sem.num] = n
                            eng.wait_ge(sem, n)
                    else:
                        inst = getattr(eng, it.meth)(*it.args, **it.kwargs)
                        for sem, k in it.incs:
                            inst.then_inc(sem, k)
            return f

        with nc.Block() as block:
            block.tensor(replay(T.items))
            block.vector(replay(V.items))
            block.scalar(replay(A.items))
            block.gpsimd(replay(P.items))
            block.sync(replay(Y.items))

    return nc


_NC_CACHE = {}


def kernel(**inputs):
    x = np.asarray(inputs["L0_attn_hidden_states"], dtype=np.float32)
    wq = np.asarray(inputs["L0_attn_Wq"], dtype=np.float32)
    wk = np.asarray(inputs["L0_attn_Wk"], dtype=np.float32)
    wv = np.asarray(inputs["L0_attn_Wv"], dtype=np.float32)
    bv = np.asarray(inputs["L0_attn_bv"], dtype=np.float32)
    wo = np.asarray(inputs["L0_attn_Wo"], dtype=np.float32)
    wup = np.asarray(inputs["L0_ffn_W_up"], dtype=np.float32)
    gw = np.asarray(inputs["L0_ffn_gate_w"], dtype=np.float32)
    eb = np.asarray(inputs["L0_ffn_expert_bias"], dtype=np.float32)
    la = np.asarray(inputs["L0_ffn_lora_A"], dtype=np.float32)
    lb = np.asarray(inputs["L0_ffn_lora_B"], dtype=np.float32)

    c = np.ascontiguousarray
    wqT = c(wq.reshape(NQ, 128, NC, 128).transpose(0, 3, 2, 1))
    wkT = c(wk.reshape(NKV, 128, NC, 128).transpose(0, 3, 2, 1))
    wvR = c(wv.reshape(NKV * 128, NC, 128).transpose(1, 2, 0))
    woT = c(wo.reshape(NC, 128, NC, 128).transpose(0, 3, 2, 1))
    wupT = c(wup.reshape(NC, 128, NC, 128).transpose(0, 3, 2, 1))
    gateT = c(gw.T.reshape(NC, 128, 4))
    ebias = c(eb.reshape(4, 1))
    aT = c(la.reshape(32, D).T.reshape(NC, 128, 32))
    bflat = c(lb.transpose(0, 2, 1).reshape(32, D))
    bvc = c(bv.reshape(NKV, 128).T)
    xT = [c(x[b].T) for b in range(2)]

    in_maps = []
    for core in range(8):
        b, s = core // 4, core % 4
        in_maps.append({
            "xTb": xT[b],
            "xTmy": c(xT[b][:, MY * s:MY * (s + 1)]),
            "wqT": wqT, "wkT": wkT, "wvR": wvR, "woT": woT, "wupT": wupT,
            "gateT": gateT, "ebias": ebias, "aT": aT, "bflat": bflat,
            "bvc": bvc,
        })

    if "nc" not in _NC_CACHE:
        _NC_CACHE["nc"] = build_nc()
    nc = _NC_CACHE["nc"]
    res = run_bass_kernel_spmd(nc, in_maps, core_ids=list(range(8)))

    out = np.empty((2, S, D), np.float32)
    for core in range(8):
        b, s = core // 4, core % 4
        out[b, MY * s:MY * (s + 1), :] = res.results[core]["outT"].T
    return out


# version 23
# speedup vs baseline: 1.0563x; 1.0563x over previous
import numpy as np
import concourse.bass as bass
import concourse.mybir as mybir
from concourse.bass_utils import run_bass_kernel_spmd

F32 = mybir.dt.float32
BF16 = mybir.dt.bfloat16
AF = mybir.ActivationFunctionType
ALU = mybir.AluOpType
AX = mybir.AxisListType

D = 4096
S = 4096
MY = 1024
NQ, NKV, DH = 32, 8, 128
NC = D // 128
QSCALE = DH ** -0.5
EPS = 1e-5
P0T = 256
NT0 = S // P0T
PRT = 256
NTP = MY // PRT
SB = 512


class Item:
    __slots__ = ("meth", "args", "kwargs", "incs")

    def __init__(self, meth, args, kwargs):
        self.meth, self.args, self.kwargs = meth, args, kwargs
        self.incs = []

    def then_inc(self, sem, k):
        self.incs.append((sem, k))
        return self


class WaitOn:
    __slots__ = ("names",)

    def __init__(self, names):
        self.names = names


class Stream:
    def __init__(self):
        self.items = []

    def __getattr__(self, meth):
        if meth.startswith("__"):
            raise AttributeError(meth)

        def f(*a, **kw):
            it = Item(meth, a, kw)
            self.items.append(it)
            return it
        return f

    def wait(self, *names):
        if names:
            self.items.append(WaitOn(names))


class Ctr:
    def __init__(self, sem, step=1):
        self.sem, self.step, self.n = sem, step, 0

    def tick(self, item):
        item.then_inc(self.sem, self.step)
        self.n += self.step
        return self.n


def build_nc():
    nc = bass.Bass("TRN2", target_bir_lowering=False)

    xTb = nc.dram_tensor("xTb", [D, S], F32, kind="ExternalInput").ap()
    xTmy = nc.dram_tensor("xTmy", [D, MY], F32, kind="ExternalInput").ap()
    wqT = nc.dram_tensor("wqT", [NQ, 128, NC, 128], F32, kind="ExternalInput").ap()
    wkT = nc.dram_tensor("wkT", [NKV, 128, NC, 128], F32, kind="ExternalInput").ap()
    wvR = nc.dram_tensor("wvR", [NC, 128, NKV * 128], F32, kind="ExternalInput").ap()
    woT = nc.dram_tensor("woT", [NC, 128, NC, 128], F32, kind="ExternalInput").ap()
    wupT = nc.dram_tensor("wupT", [NC, 128, NC, 128], F32, kind="ExternalInput").ap()
    gateT = nc.dram_tensor("gateT", [NC, 128, 4], F32, kind="ExternalInput").ap()
    ebias = nc.dram_tensor("ebias", [4, 1], F32, kind="ExternalInput").ap()
    aT = nc.dram_tensor("aT", [NC, 128, 32], F32, kind="ExternalInput").ap()
    bflat = nc.dram_tensor("bflat", [32, D], F32, kind="ExternalInput").ap()
    bvc = nc.dram_tensor("bvc", [128, NKV], F32, kind="ExternalInput").ap()
    outT = nc.dram_tensor("outT", [D, MY], F32, kind="ExternalOutput").ap()

    k_d = nc.dram_tensor("k_d", [NKV, 128, S], BF16, kind="Internal").ap()
    xnst = nc.dram_tensor("xnst", [128, NC, MY], BF16, kind="Internal").ap()
    v_d = nc.dram_tensor("v_d", [NKV, S // 128, 128, 128], BF16, kind="Internal").ap()

    from contextlib import ExitStack
    with ExitStack() as top:
        ones_col = top.enter_context(nc.sbuf_tensor("ones_col", [128, 1], BF16))
        ones_row = top.enter_context(nc.sbuf_tensor("ones_row", [1, 128], F32))
        cst_eps = top.enter_context(nc.sbuf_tensor("cst_eps", [1, 1], F32))
        eb_sb = top.enter_context(nc.sbuf_tensor("eb_sb", [4, 1], F32))
        bvc_sb = top.enter_context(nc.sbuf_tensor("bvc_sb", [128, NKV], F32))
        idn = top.enter_context(nc.sbuf_tensor("idn", [128, 128], F32))
        e4 = top.enter_context(nc.sbuf_tensor("e4", [4, 32], BF16))
        gate_sb = top.enter_context(nc.sbuf_tensor("gate_sb", [128, NC, 4], BF16))
        m_row = top.enter_context(nc.sbuf_tensor("m_row", [1, P0T], F32))
        v_row = top.enter_context(nc.sbuf_tensor("v_row", [1, P0T], F32))
        r_row = top.enter_context(nc.sbuf_tensor("r_row", [1, P0T], F32))

        pb = [top.enter_context(nc.psum_tensor(f"pb{i}", [128, 512], F32))
              for i in range(8)]

        s0 = ExitStack()
        x0 = [s0.enter_context(nc.sbuf_tensor(f"x0_{i}", [128, NC, P0T], BF16))
              for i in range(2)]
        wk_sb = s0.enter_context(nc.sbuf_tensor("wk_sb", [128, NKV, NC, 128], BF16))
        wv_sb = s0.enter_context(nc.sbuf_tensor("wv_sb", [128, NC, NKV * 128], BF16))
        xn0 = [s0.enter_context(nc.sbuf_tensor(f"xn0_{i}", [128, NC, P0T], BF16))
               for i in range(2)]
        x20 = s0.enter_context(nc.sbuf_tensor("x20", [128, 2, P0T], BF16))
        xf0 = s0.enter_context(nc.sbuf_tensor("xf0", [128, 2, P0T], F32))
        kt0 = [s0.enter_context(nc.sbuf_tensor(f"kt0_{i}", [128, P0T], BF16))
               for i in range(2)]
        vt0 = [s0.enter_context(nc.sbuf_tensor(f"vt0_{i}", [128, NKV * 128], BF16))
               for i in range(2)]
        s0.close()

        xn_my = top.enter_context(nc.sbuf_tensor("xn_my", [128, NC, MY], BF16))
        ctx_sb = top.enter_context(nc.sbuf_tensor("ctx_sb", [128, NC, MY], BF16))

        s1 = ExitStack()
        wq_sb = [s1.enter_context(nc.sbuf_tensor(f"wq_sb{i}", [128, NC, 128], BF16))
                 for i in range(2)]
        kh_sb = [s1.enter_context(nc.sbuf_tensor(f"kh_sb{i}", [128, S], BF16))
                 for i in range(2)]
        vh_sb = [s1.enter_context(nc.sbuf_tensor(f"vh_sb{i}", [128, S // 128, 128], BF16))
                 for i in range(2)]
        q_sb = s1.enter_context(nc.sbuf_tensor("q_sb", [128, MY], BF16))
        expp = s1.enter_context(nc.sbuf_tensor("expp", [128, 4, SB], BF16))
        dpair = s1.enter_context(nc.sbuf_tensor("dpair", [128, 10, SB], BF16))
        recip_b = s1.enter_context(nc.sbuf_tensor("recip_b", [128, SB], F32))
        ct_f = s1.enter_context(nc.sbuf_tensor("ct_f", [128, SB], F32))
        den_row = s1.enter_context(nc.sbuf_tensor("den_row", [1, SB], F32))
        s1.close()

        s2 = ExitStack()
        wo_sb = [s2.enter_context(nc.sbuf_tensor(f"wo_sb{i}", [128, NC, 128], BF16))
                 for i in range(2)]
        wu_sb = [s2.enter_context(nc.sbuf_tensor(f"wu_sb{i}", [128, NC, 128], BF16))
                 for i in range(2)]
        xres = [s2.enter_context(nc.sbuf_tensor(f"xres{i}", [128, MY], F32))
                for i in range(2)]
        at_sb = s2.enter_context(nc.sbuf_tensor("at_sb", [128, NC, 32], BF16))
        bf_sb = s2.enter_context(nc.sbuf_tensor("bf_sb", [32, D], BF16))
        lg_sb = s2.enter_context(nc.sbuf_tensor("lg_sb", [4, MY], F32))
        lgT_sb = s2.enter_context(nc.sbuf_tensor("lgT_sb", [128, MY // 128, 4], F32))
        mx_sb = s2.enter_context(nc.sbuf_tensor("mx_sb", [128, MY // 128, 1], F32))
        oh_sb = s2.enter_context(nc.sbuf_tensor("oh_sb", [128, MY // 128, 4], F32))
        ohT_sb = s2.enter_context(nc.sbuf_tensor("ohT_sb", [4, MY], BF16))
        m32_sb = s2.enter_context(nc.sbuf_tensor("m32_sb", [32, MY], BF16))
        zsel_sb = s2.enter_context(nc.sbuf_tensor("zsel_sb", [32, MY], BF16))
        g_sb = s2.enter_context(nc.sbuf_tensor("g_sb", [128, SB], F32))
        t_sb = s2.enter_context(nc.sbuf_tensor("t_sb", [128, SB], F32))
        ot_sb = [s2.enter_context(nc.sbuf_tensor(f"ot_sb{i}", [128, MY], F32))
                 for i in range(2)]
        s2.close()

        sPE = Ctr(top.enter_context(nc.semaphore("sPE")))
        sDVE = Ctr(top.enter_context(nc.semaphore("sDVE")))
        sACT = Ctr(top.enter_context(nc.semaphore("sACT")))
        sPOOL = Ctr(top.enter_context(nc.semaphore("sPOOL")))
        def dsem(name):
            return Ctr(top.enter_context(nc.semaphore(name)), 16)

        qW0 = dsem("qW0")
        qX = dsem("qX")
        qK = [dsem("qK0"), dsem("qK1")]
        qV = [dsem("qV0"), dsem("qV1")]
        qQ = [dsem("qQ0"), dsem("qQ1")]
        qWo = [dsem("qWo0"), dsem("qWo1")]
        qWu = [dsem("qWu0"), dsem("qWu1")]
        qWb = dsem("qWb")
        qR = [dsem("qR0"), dsem("qR1")]
        qSK = [dsem("qSK0"), dsem("qSK1")]
        qSV = [dsem("qSV0"), dsem("qSV1")]
        qSO = [dsem("qSO0"), dsem("qSO1")]
        qXS = [dsem("qXS0"), dsem("qXS1")]
        qXM = dsem("qXM")
        qWK = [dsem(f"qWK{h}") for h in range(NKV)]
        qWV = [dsem("qWVa"), dsem("qWVb")]
        all_q = ([qW0, qX, qWb, qXM] + qK + qV + qQ + qWo + qWu + qR
                 + qSK + qSV + qSO + qXS + qWK + qWV)

        K = {}

        def mark(name, ctr, item):
            K[name] = (ctr.sem, ctr.tick(item))

        T, V, A, P, Y = Stream(), Stream(), Stream(), Stream(), Stream()

        sPOOL.tick(P.memset(ones_col[:], 1.0))
        sPOOL.tick(P.memset(ones_row[:], 1.0))
        sPOOL.tick(P.memset(cst_eps[:], EPS))
        mark("idn0", sPOOL, P.memset(idn[:], 0.0))
        P.wait("idn0")
        mark("idn", sPOOL,
             P.affine_select(out=idn[:], in_=idn[:], compare_op=ALU.not_equal,
                             fill=1.0, base=0, pattern=[[-1, 128]],
                             channel_multiplier=1))
        mark("e40", sPOOL, P.memset(e4[:], 1.0))
        P.wait("e40")
        mark("e41", sPOOL, P.affine_select(
            e4[:], e4[:], pattern=[[1, 32]], channel_multiplier=-8,
            base=0, compare_op=ALU.is_ge, fill=0.0))
        P.wait("e41")
        mark("consts", sPOOL, P.affine_select(
            e4[:], e4[:], pattern=[[-1, 32]], channel_multiplier=8,
            base=7, compare_op=ALU.is_ge, fill=0.0))
        K["e4done"] = K["consts"]
        qW0.tick(P.dma_start(eb_sb[:], ebias))
        qW0.tick(P.dma_start(bvc_sb[:], bvc))
        mark("gws", qW0, P.dma_start(gate_sb[:], gateT.rearrange("c p e -> p c e")))

        chunk_par = [0]

        def ln_tile(tag, xsrc, xn_dst, n_tok, first_waits, prev=None,
                    norm_first_waits=(), bcp=(2, 3)):
            ps_sum = pb[0][0:1, 0:n_tok]
            ps_sq = pb[1][0:1, 0:n_tok]
            ps_mb = pb[bcp[0]][:, 0:n_tok]
            ps_rb = pb[bcp[1]][:, 0:n_tok]
            for c in range(NC):
                j = chunk_par[0]
                chunk_par[0] += 1
                sl = j % 2
                V.wait(f"{tag}src" if c == 0 else None) if c == 0 else None
                if c == 0:
                    V.wait(f"{tag}src")
                if j >= 2:
                    V.wait(f"lnB_{j-2}")
                mark(f"lnq_{j}", sDVE,
                     V.tensor_mul(x20[:, sl, 0:n_tok], xsrc[:, c, 0:n_tok],
                                  xsrc[:, c, 0:n_tok]))
                T.wait(f"lnq_{j}")
                if c == 0:
                    T.wait(f"{tag}src", *first_waits)
                T.matmul(ps_sum, ones_col[:], xsrc[:, c, 0:n_tok],
                         start=(c == 0), stop=(c == NC - 1))
                i2 = T.matmul(ps_sq, ones_col[:], x20[:, sl, 0:n_tok],
                              start=(c == 0), stop=(c == NC - 1))
                mark(f"lnB_{j}", sPE, i2)
            jl = chunk_par[0] - 1
            V.wait(f"lnB_{jl}")
            if prev is not None:
                V.wait(f"{prev}bm", f"{prev}bc", f"{prev}ln")
            mark(f"{tag}m", sDVE,
                 V.tensor_scalar_mul(m_row[:, 0:n_tok], ps_sum, 1.0 / D))
            mark(f"{tag}v0", sDVE,
                 V.tensor_scalar_mul(v_row[:, 0:n_tok], ps_sq, 1.0 / D))
            V.wait(f"{tag}m", f"{tag}v0")
            mark(f"{tag}m2", sDVE,
                 V.tensor_mul(r_row[:, 0:n_tok], m_row[:, 0:n_tok],
                              m_row[:, 0:n_tok]))
            V.wait(f"{tag}m2")
            mark(f"{tag}var", sDVE,
                 V.tensor_sub(v_row[:, 0:n_tok], v_row[:, 0:n_tok],
                              r_row[:, 0:n_tok]))
            A.wait(f"{tag}var")
            if prev is not None:
                A.wait(f"{prev}bc")
            mark(f"{tag}ln", sACT,
                 A.activation(r_row[:, 0:n_tok], v_row[:, 0:n_tok], AF.Ln,
                              bias=cst_eps[:, :]))
            A.wait(f"{tag}ln")
            mark(f"{tag}rstd", sACT,
                 A.activation(r_row[:, 0:n_tok], r_row[:, 0:n_tok], AF.Exp,
                              scale=-0.5))
            T.wait(f"{tag}m")
            mark(f"{tag}bm", sPE,
                 T.matmul(ps_mb, ones_row[:], m_row[:, 0:n_tok],
                          start=True, stop=True))
            T.wait(f"{tag}rstd")
            mark(f"{tag}bc", sPE,
                 T.matmul(ps_rb, ones_row[:], r_row[:, 0:n_tok],
                          start=True, stop=True))
            for c in range(NC):
                sl = c % 2
                V.wait(f"{tag}bc")
                if c == 0:
                    V.wait(*norm_first_waits)
                mark(f"{tag}sub{c}", sDVE,
                     V.tensor_sub(xf0[:, sl, 0:n_tok], xsrc[:, c, 0:n_tok], ps_mb))
                V.wait(f"{tag}sub{c}")
                mark(f"{tag}xn{c}", sDVE,
                     V.tensor_tensor(xn_dst(c), xf0[:, sl, 0:n_tok], ps_rb,
                                     ALU.mult))


        def kv_emit(tt, xtag):
            for h in range(NKV):
                hh = tt * NKV + h
                if tt == 0:
                    T.wait(f"wkh_{h}")
                if hh > 0:
                    T.wait(f"kev_{hh-1}")
                if h == 0 and tt > 0:
                    T.wait(f"vev_{tt*2-1}_1")
                for c in range(NC):
                    if h == 0:
                        T.wait(f"{xtag}xn{c}")
                    i = T.matmul(pb[4][:, 0:P0T], wk_sb[:, h, c, :],
                                 xn0[tt % 2][:, c, :],
                                 start=(c == 0), stop=(c == NC - 1))
                mark(f"kmm_{hh}", sPE, i)
                A.wait(f"kmm_{hh}")
                if hh >= 2:
                    A.wait(f"kst_{hh-2}")
                mark(f"kev_{hh}", sACT,
                     A.activation(kt0[h % 2][:], pb[4][:, 0:P0T], AF.Copy))
                A.wait(f"kev_{hh}")
                mark(f"kst_{hh}", qSK[hh % 2],
                     A.dma_start(k_d[h, :, tt * P0T:(tt + 1) * P0T], kt0[h % 2][:]))
            for tc in range(P0T // 128):
                tcg = tt * (P0T // 128) + tc
                if tcg > 0:
                    T.wait(f"vev_{tcg-1}_0", f"vev_{tcg-1}_1")
                T.wait(f"kev_{tt*NKV+NKV-1}")
                for half in range(2):
                    ps = pb[5 if half == 0 else 4][:, 0:512]
                    if tt == 0 and tc == 0:
                        T.wait("wva" if half == 0 else "w0")
                    for c in range(NC):
                        i = T.matmul(ps, xn0[tt % 2][:, c, tc * 128:(tc + 1) * 128],
                                     wv_sb[:, c, half * 512:(half + 1) * 512],
                                     start=(c == 0), stop=(c == NC - 1))
                    mark(f"vmm_{tcg}_{half}", sPE, i)
                for half in range(2):
                    A.wait(f"vmm_{tcg}_{half}")
                    if tcg >= 2:
                        A.wait(f"vst_{tcg-2}")
                    mark(f"vev_{tcg}_{half}", sACT,
                         A.activation(vt0[tc % 2][:, half * 512:(half + 1) * 512],
                                      pb[5 if half == 0 else 4][:, 0:512], AF.Copy))
                A.wait(f"vev_{tcg}_0", f"vev_{tcg}_1")
                for h in range(NKV):
                    i = A.dma_start(v_d[h, tcg], vt0[tc % 2][:, h * 128:(h + 1) * 128])
                    if h < NKV - 1:
                        qSV[tcg % 2].tick(i)
                mark(f"vst_{tcg}", qSV[tcg % 2], i)

        seq = []
        bi = 0
        for i4 in range(4):
            seq += [("b", bi), ("b", bi + 1), ("b", bi + 2), ("b", bi + 3)]
            seq.append(("p", i4))
            bi += 4
        kvq = []
        done_kv = 0
        for sidx, (kind, ti) in enumerate(seq):
            tg = f"t{sidx}"
            if sidx >= 2:
                P.wait(f"t{sidx-2}xn{NC-1}", f"lnB_{(sidx-1)*NC-1}")
                if seq[sidx - 2][0] == "p":
                    P.wait(f"xst_{seq[sidx-2][1]}")
            if sidx >= 1:
                P.wait(f"lnq_{(sidx-1)*NC}")
            if kind == "b":
                src_ap = xTb.rearrange("(c p) t -> p c t", p=128)[
                    :, :, ti * P0T:(ti + 1) * P0T]
            else:
                src_ap = xTmy.rearrange("(c p) t -> p c t", p=128)[
                    :, :, ti * P0T:(ti + 1) * P0T]
            mark(f"{tg}src", qX, P.dma_start(x0[sidx % 2][:], src_ap))
            if sidx == 1:
                for h in range(2):
                    mark(f"wkh_{h}", qWK[h],
                         P.dma_start(wk_sb[:, h], wkT[h]))
            elif sidx == 2:
                mark("wva", qWV[0],
                     P.dma_start(wv_sb[:, :, 0:512],
                                 wvR.rearrange("c p v -> p c v")[:, :, 0:512]))
                for h in range(2, 4):
                    mark(f"wkh_{h}", qWK[h],
                         P.dma_start(wk_sb[:, h], wkT[h]))
            elif sidx == 3:
                mark("w0", qWV[1],
                     P.dma_start(wv_sb[:, :, 512:1024],
                                 wvR.rearrange("c p v -> p c v")[:, :, 512:1024]))
                for h in range(4, NKV):
                    mark(f"wkh_{h}", qWK[h],
                         P.dma_start(wk_sb[:, h], wkT[h]))
            fw = ["consts"] if sidx == 0 else [f"t{sidx-1}m", f"t{sidx-1}v0"]
            if sidx >= 2:
                fw.append(f"t{sidx-2}xn{NC-1}")
            if kind == "b":
                nfw = [f"vmm_{(ti-2)*2+1}_1"] if ti >= 2 else ()
                xnd = (lambda c, sl=ti % 2: xn0[sl][:, c, 0:P0T])
            else:
                nfw = ()
                xnd = (lambda c, sl=sidx % 2: x0[sl][:, c, 0:P0T])
            ln_tile(tg, x0[sidx % 2], xnd, P0T, fw,
                    prev=(f"t{sidx-1}" if sidx > 0 else None),
                    norm_first_waits=nfw,
                    bcp=((2, 3) if sidx % 2 == 0 else (6, 7)))
            if kind == "b":
                kvq.append((ti, tg, sidx))
            else:
                for c in range(NC):
                    Y.wait(f"{tg}xn{c}")
                    i = Y.dma_start(xnst[:, c, ti * P0T:(ti + 1) * P0T],
                                    x0[sidx % 2][:, c, :])
                    if c < NC - 1:
                        qXS[ti % 2].tick(i)
                mark(f"xst_{ti}", qXS[ti % 2], i)
            if kvq and kvq[0][2] < sidx:
                bti, btg, _ = kvq.pop(0)
                kv_emit(bti, btg)
        while kvq:
            bti, btg, _ = kvq.pop(0)
            kv_emit(bti, btg)

        k_done = (f"kst_{NT0*NKV-2}", f"kst_{NT0*NKV-1}")
        v_done = (f"vst_{NT0*(P0T//128)-2}", f"vst_{NT0*(P0T//128)-1}")
        p0_pe_done = f"vmm_{NT0*(P0T//128)-1}_1"

        Y.wait("xst_0", "xst_1", "xst_2", "xst_3",
               f"t{len(seq)-1}xn{NC-1}", f"lnB_{len(seq)*NC-1}",
               *k_done, *v_done, p0_pe_done)
        mark("xnmy", qXM, Y.dma_start(xn_my[:], xnst))

        pro_done = "xnmy"
        pro_rows = [f"t{NT0+NTP-1}m", f"t{NT0+NTP-1}v0"]

        for h in range(NKV):
            P.wait(*k_done, *v_done)
            if h >= 2:
                P.wait(f"qk_{4*(h-2)+3}_1_31")
            mark(f"kh_{h}", qK[h % 2], P.dma_start(kh_sb[h % 2][:], k_d[h]))
            if h >= 2:
                P.wait(f"pvl_{4*(h-2)+3}_1")
            mark(f"vh_{h}", qV[h % 2],
                 P.dma_start(vh_sb[h % 2][:],
                             v_d[h].rearrange("c p d -> p c d")))
            for g in range(4):
                qh = h * 4 + g
                if qh < 2:
                    P.wait(pro_done, *pro_rows, *k_done, *v_done, p0_pe_done)
                else:
                    P.wait(f"qmm_{qh-2}_1")
                mark(f"wq_{qh}", qQ[qh % 2], P.dma_start(wq_sb[qh % 2][:], wqT[qh]))

        gchunk = [0]

        def qproj_emit(qh, interleave=None):
            if interleave is None:
                T.wait(f"wq_{qh}")
                if qh == 0:
                    T.wait(pro_done, *pro_rows)
                for sb2 in range(2):
                    for c in range(NC):
                        i = T.matmul(pb[sb2][:, 0:512], wq_sb[qh % 2][:, c, :],
                                     xn_my[:, c, sb2 * 512:(sb2 + 1) * 512],
                                     start=(c == 0), stop=(c == NC - 1))
                    mark(f"qmm_{qh}_{sb2}", sPE, i)
                return
            it = interleave
            sb2, c = it // NC, it % NC
            if it == 0:
                T.wait(f"wq_{qh}")
                T.wait(f"qev_{qh-1}_0", f"qev_{qh-1}_1")
            i = T.matmul(pb[sb2][:, 0:512], wq_sb[qh % 2][:, c, :],
                         xn_my[:, c, sb2 * 512:(sb2 + 1) * 512],
                         start=(c == 0), stop=(c == NC - 1))
            if c == NC - 1:
                mark(f"qmm_{qh}_{sb2}", sPE, i)

        def qev_emit(qh):
            for sb2 in range(2):
                A.wait(f"qmm_{qh}_{sb2}")
                if qh > 0:
                    A.wait(f"qk_{qh-1}_1_31")
                mark(f"qev_{qh}_{sb2}", sACT,
                     A.activation(q_sb[:, sb2 * 512:(sb2 + 1) * 512],
                                  pb[sb2][:, 0:512], AF.Copy, scale=QSCALE))

        qproj_emit(0)
        qev_emit(0)
        pending_bc = [None]
        pending_vt = [None]
        for kvh in range(NKV):
            for g in range(4):
                qh = kvh * 4 + g
                for sb in range(2):
                    jbase = gchunk[0]
                    T.wait(f"qev_{qh}_{sb}", f"kh_{kvh}")
                    SCB = (2, 3, 7)
                    for c in range(3):
                        j = jbase + c
                        if j >= 3:
                            T.wait(f"exp_{j-3}")
                        mark(f"qk_{qh}_{sb}_{c}", sPE,
                             T.matmul(pb[SCB[j % 3]][:, 0:SB],
                                      kh_sb[kvh % 2][:, c * 128:(c + 1) * 128],
                                      q_sb[:, sb * 512:(sb + 1) * 512],
                                      start=True, stop=True))
                    for c in range(NC):
                        j = jbase + c
                        A.wait(f"qk_{qh}_{sb}_{c}")
                        if j >= 4:
                            A.wait(f"pr_{(j-4)//2}")
                        mark(f"exp_{j}", sACT,
                             A.activation(expp[:, j % 4, :], pb[SCB[j % 3]][:, 0:SB],
                                          AF.Exp))
                        T.wait(f"exp_{j}")
                        if c == 0:
                            T.wait(f"vh_{kvh}")
                            if qh > 0 or sb > 0:
                                pq, psb = (qh, 0) if sb else (qh - 1, 1)
                                T.wait(f"ctxm_{pq}_{psb}")
                        i = T.matmul(pb[4][:, 0:SB], vh_sb[kvh % 2][:, c, :],
                                     expp[:, j % 4, :],
                                     start=(c == 0), stop=(c == NC - 1))
                        if c == NC - 1:
                            mark(f"pvl_{qh}_{sb}", sPE, i)
                        if qh < NQ - 1:
                            qproj_emit(qh + 1, interleave=sb * NC + c)
                        if c % 2 == 1:
                            V.wait(f"exp_{j}")
                            if qh == 0 and sb == 0 and c == 1:
                                V.wait(f"vev_{NT0*(P0T//128)-1}_1")
                            if j // 2 >= 2:
                                V.wait(f"qd_{(j-4)//4}")
                            mark(f"pr_{j//2}", sDVE,
                                 V.tensor_add(dpair[:, 8 + ((c - 1) // 2) % 2, :],
                                              expp[:, (j - 1) % 4, :],
                                              expp[:, j % 4, :]))
                        if c % 4 == 3:
                            V.wait(f"pr_{j//2}", f"pr_{(j-2)//2}")
                            if c == 3 and (qh > 0 or sb > 0):
                                V.wait(f"den_{jbase-1}")
                            mark(f"qd_{j//4}", sDVE,
                                 V.tensor_add(dpair[:, (c - 3) // 4, :],
                                              dpair[:, 8, :],
                                              dpair[:, 9, :]))
                        if c == 5 and pending_vt[0] is not None:
                            V.wait("gws")
                            pq, psb, pkv = pending_vt[0]
                            V.wait(f"bc_{pq}_{psb}")
                            mark(f"rc_{pq}_{psb}", sDVE,
                                 V.tensor_copy(recip_b[:], pb[6][:, 0:SB]))
                            V.wait(f"rc_{pq}_{psb}")
                            mark(f"ctxn_{pq}_{psb}", sDVE,
                                 V.tensor_tensor(ct_f[:], ct_f[:], recip_b[:],
                                                 ALU.mult))
                            V.wait(f"ctxn_{pq}_{psb}")
                            mark(f"ctxw_{pq}_{psb}", sDVE,
                                 V.tensor_scalar(
                                     ctx_sb[:, pq, psb * 512:(psb + 1) * 512],
                                     ct_f[:], bvc_sb[:, pkv:pkv + 1], None,
                                     op0=ALU.add))
                            pending_vt[0] = None
                        if c == 3 and pending_bc[0] is not None:
                            pq, psb = pending_bc[0]
                            T.wait(f"dre_{pq}_{psb}")
                            if pq > 0 or psb > 0:
                                p2q, p2sb = (pq, 0) if psb else (pq - 1, 1)
                                T.wait(f"rc_{p2q}_{p2sb}")
                            mark(f"bc_{pq}_{psb}", sPE,
                                 T.matmul(pb[6][:, 0:SB], ones_row[:], den_row[:],
                                          start=True, stop=True))
                            pending_bc[0] = None
                        if c + 3 < NC:
                            j2 = jbase + c + 3
                            T.wait(f"exp_{j2-3}")
                            mark(f"qk_{qh}_{sb}_{c+3}", sPE,
                                 T.matmul(pb[SCB[j2 % 3]][:, 0:SB],
                                          kh_sb[kvh % 2][:, (c + 2 + 1) * 128:(c + 4) * 128],
                                          q_sb[:, sb * 512:(sb + 1) * 512],
                                          start=True, stop=True))
                    if qh > 0 or sb > 0:
                        pq, psb = (qh, 0) if sb else (qh - 1, 1)
                        T.wait(f"denr_{pq}_{psb}")
                    for i2 in range(NC // 4):
                        T.wait(f"qd_{(jbase + 4*i2 + 3)//4}")
                        di = T.matmul(pb[5][0:1, 0:SB], ones_col[:],
                                      dpair[:, i2, :],
                                      start=(i2 == 0), stop=(i2 == NC // 4 - 1))
                    mark(f"den_{jbase+NC-1}", sPE, di)
                    gchunk[0] += NC
                    V.wait(f"pvl_{qh}_{sb}")
                    if qh > 0 or sb > 0:
                        pq, psb = (qh, 0) if sb else (qh - 1, 1)
                        V.wait(f"ctxw_{pq}_{psb}")
                    mark(f"ctxm_{qh}_{sb}", sDVE,
                         V.tensor_copy(ct_f[:], pb[4][:, 0:SB]))
                    V.wait(f"den_{jbase+NC-1}")
                    if qh > 0 or sb > 0:
                        pq, psb = (qh, 0) if sb else (qh - 1, 1)
                        V.wait(f"bc_{pq}_{psb}")
                        V.wait(f"dre_{pq}_{psb}")
                    mark(f"denr_{qh}_{sb}", sDVE,
                         V.tensor_copy(den_row[:], pb[5][0:1, 0:SB]))
                    A.wait(f"denr_{qh}_{sb}")
                    mark(f"dln_{qh}_{sb}", sACT,
                         A.activation(den_row[:], den_row[:], AF.Ln))
                    A.wait(f"dln_{qh}_{sb}")
                    mark(f"dre_{qh}_{sb}", sACT,
                         A.activation(den_row[:], den_row[:], AF.Exp, scale=-1.0))
                    pending_bc[0] = (qh, sb)
                    pending_vt[0] = (qh, sb, kvh)
                if qh < NQ - 1:
                    qev_emit(qh + 1)

        pq, psb = pending_bc[0]
        T.wait(f"dre_{pq}_{psb}")
        p2q, p2sb = (pq, 0) if psb else (pq - 1, 1)
        T.wait(f"rc_{p2q}_{p2sb}")
        mark(f"bc_{pq}_{psb}", sPE,
             T.matmul(pb[6][:, 0:SB], ones_row[:], den_row[:],
                      start=True, stop=True))
        pending_bc[0] = None
        pq, psb, pkv = pending_vt[0]
        V.wait("gws")
        V.wait(f"bc_{pq}_{psb}")
        mark(f"rc_{pq}_{psb}", sDVE, V.tensor_copy(recip_b[:], pb[6][:, 0:SB]))
        V.wait(f"rc_{pq}_{psb}")
        mark(f"ctxn_{pq}_{psb}", sDVE,
             V.tensor_tensor(ct_f[:], ct_f[:], recip_b[:], ALU.mult))
        V.wait(f"ctxn_{pq}_{psb}")
        mark(f"ctxw_{pq}_{psb}", sDVE,
             V.tensor_scalar(ctx_sb[:, pq, psb * 512:(psb + 1) * 512],
                             ct_f[:], bvc_sb[:, pkv:pkv + 1], None, op0=ALU.add))
        pending_vt[0] = None

        att_done = f"ctxw_{NQ-1}_1"
        att_pe_done = f"den_{gchunk[0]-1}"

        P.wait(att_done, att_pe_done)
        qWb.tick(P.dma_start(at_sb[:], aT.rearrange("c p e -> p c e")))
        mark("w1b", qWb, P.dma_start(bf_sb[:], bflat))
        for oc in range(NC):
            if oc >= 2:
                P.wait(f"womm_{oc-2}_1")
            mark(f"wo_{oc}", qWo[oc % 2], P.dma_start(wo_sb[oc % 2][:], woT[oc]))
            if oc >= 2:
                P.wait(f"ffnm_{oc-2}_1")
            mark(f"wu_{oc}", qWu[oc % 2], P.dma_start(wu_sb[oc % 2][:], wupT[oc]))
            if oc < 2:
                Y.wait(att_done)
            else:
                Y.wait(f"ow_{oc-2}_1")
            mark(f"xr_{oc}", qR[oc % 2],
                 Y.dma_start(xres[oc % 2][:], xTmy[oc * 128:(oc + 1) * 128, :]))

        def womm_emit(oc):
            T.wait(f"wo_{oc}")
            if oc >= 2:
                T.wait(f"tm_{oc-2}_0", f"tm_{oc-2}_1")
            pp = 4 if oc % 2 == 0 else 6
            for sb in range(2):
                for c in range(NC):
                    i = T.matmul(pb[pp + sb][:, 0:512], wo_sb[oc % 2][:, c, :],
                                 ctx_sb[:, c, sb * 512:(sb + 1) * 512],
                                 start=(c == 0), stop=(c == NC - 1))
                mark(f"womm_{oc}_{sb}", sPE, i)

        def ffnm_emit(oc):
            T.wait(f"wu_{oc}", "zs_0", "zs_1")
            if oc >= 2:
                T.wait(f"gev_{oc-2}_1")
            elif oc == 0:
                T.wait("lgev_0", "lgev_1")
            qp = 0 if oc % 2 == 0 else 2
            for sb in range(2):
                for c in range(NC):
                    T.matmul(pb[qp + sb][:, 0:512], wu_sb[oc % 2][:, c, :],
                             xn_my[:, c, sb * 512:(sb + 1) * 512],
                             start=(c == 0), stop=False)
                mark(f"ffnm_{oc}_{sb}", sPE,
                     T.matmul(pb[qp + sb][:, 0:512],
                              bf_sb[:, oc * 128:(oc + 1) * 128],
                              zsel_sb[:, sb * 512:(sb + 1) * 512],
                              start=False, stop=True))

        def tail_emit(oc):
            pp = 4 if oc % 2 == 0 else 6
            qp = 0 if oc % 2 == 0 else 2
            for sb in range(2):
                A.wait(f"ffnm_{oc}_{sb}")
                if oc > 0 or sb > 0:
                    poc, psb = (oc, 0) if sb else (oc - 1, 1)
                    A.wait(f"tm_{poc}_{psb}")
                mark(f"gev_{oc}_{sb}", sACT,
                     A.activation(g_sb[:], pb[qp + sb][:, 0:512], AF.Silu))
                V.wait(f"womm_{oc}_{sb}", f"gev_{oc}_{sb}")
                if oc > 0 or sb > 0:
                    poc, psb = (oc, 0) if sb else (oc - 1, 1)
                    V.wait(f"ow_{poc}_{psb}")
                mark(f"tm_{oc}_{sb}", sDVE,
                     V.tensor_tensor(t_sb[:], pb[pp + sb][:, 0:512],
                                     g_sb[:], ALU.mult))
                V.wait(f"tm_{oc}_{sb}", f"xr_{oc}")
                if oc >= 2:
                    V.wait(f"ost_{oc-2}")
                mark(f"ow_{oc}_{sb}", sDVE,
                     V.tensor_add(ot_sb[oc % 2][:, sb * 512:(sb + 1) * 512],
                                  t_sb[:],
                                  xres[oc % 2][:, sb * 512:(sb + 1) * 512]))
            A.wait(f"ow_{oc}_0", f"ow_{oc}_1")
            mark(f"ost_{oc}", qSO[oc % 2],
                 A.dma_start(outT[oc * 128:(oc + 1) * 128, :], ot_sb[oc % 2][:]))

        T.wait(att_done, att_pe_done, "gws")
        for sb in range(2):
            T.wait(f"qev_{NQ-1}_{sb}")
            for c in range(NC):
                i = T.matmul(pb[sb][0:4, 0:512], gate_sb[:, c, :],
                             xn_my[:, c, sb * 512:(sb + 1) * 512],
                             start=(c == 0), stop=(c == NC - 1))
            mark(f"lgmm_{sb}", sPE, i)
            V.wait(f"lgmm_{sb}", att_done, "gws")
            mark(f"lgev_{sb}", sDVE,
                 V.tensor_scalar(lg_sb[:, sb * 512:(sb + 1) * 512],
                                 pb[sb][0:4, 0:512], eb_sb[:, 0:1], None,
                                 op0=ALU.add))
        womm_emit(0)
        womm_emit(1)
        for b in range(MY // 128):
            T.wait(f"lgev_{b // 4}", "idn")
            if b > 0:
                T.wait(f"lgtc_{b-1}")
            mark(f"lgt_{b}", sPE,
                 T.transpose(pb[2][:, 0:4], lg_sb[:, b * 128:(b + 1) * 128],
                             idn[0:4, 0:4]))
            V.wait(f"lgt_{b}")
            mark(f"lgtc_{b}", sDVE, V.tensor_copy(lgT_sb[:, b, :], pb[2][:, 0:4]))
        for b in range(MY // 128):
            V.wait(f"lgtc_{b}")
            mark(f"mx_{b}", sDVE,
                 V.reduce_max(mx_sb[:, b, :], lgT_sb[:, b, :], axis=AX.X))
            V.wait(f"mx_{b}")
            mark(f"oh_{b}", sDVE,
                 V.tensor_scalar(oh_sb[:, b, :], lgT_sb[:, b, :], mx_sb[:, b, 0:1],
                                 None, op0=ALU.is_ge))
        for b in range(MY // 128):
            T.wait(f"oh_{b}", f"lgtc_{MY//128-1}")
            if b > 0:
                T.wait(f"ohtc_{b-1}")
            mark(f"oht_{b}", sPE,
                 T.transpose(pb[2][0:4, 0:128], oh_sb[:, b, :], idn[:, :]))
            V.wait(f"oht_{b}")
            mark(f"ohtc_{b}", sDVE,
                 V.tensor_copy(ohT_sb[:, b * 128:(b + 1) * 128], pb[2][0:4, 0:128]))
        for sb in range(2):
            T.wait(f"ohtc_{MY//128-1}", "e4done")
            if sb > 0:
                T.wait("m32e_0")
            mark(f"m32m_{sb}", sPE,
                 T.matmul(pb[2][0:32, 0:512], e4[:],
                          ohT_sb[:, sb * 512:(sb + 1) * 512], start=True, stop=True))
            V.wait(f"m32m_{sb}")
            mark(f"m32e_{sb}", sDVE,
                 V.tensor_copy(m32_sb[:, sb * 512:(sb + 1) * 512],
                               pb[2][0:32, 0:512]))
        for sb in range(2):
            if sb > 0:
                T.wait("zs_0")
            else:
                T.wait("w1b")
            for c in range(NC):
                i = T.matmul(pb[3][0:32, 0:512], at_sb[:, c, :],
                             xn_my[:, c, sb * 512:(sb + 1) * 512],
                             start=(c == 0), stop=(c == NC - 1))
            mark(f"zmm_{sb}", sPE, i)
            V.wait(f"zmm_{sb}", f"m32e_{sb}")
            mark(f"zs_{sb}", sDVE,
                 V.tensor_tensor(zsel_sb[:, sb * 512:(sb + 1) * 512],
                                 pb[3][0:32, 0:512],
                                 m32_sb[:, sb * 512:(sb + 1) * 512], ALU.mult))
        ffnm_emit(0)
        tail_emit(0)
        ffnm_emit(1)
        tail_emit(1)
        for oc in range(2, NC):
            womm_emit(oc)
            ffnm_emit(oc)
            tail_emit(oc)

        Y.wait(f"ost_{NC-2}", f"ost_{NC-1}")
        Y.items.append(WaitOn(tuple(f"__q{i}__" for i in range(len(all_q)))))

        for i, q in enumerate(all_q):
            K[f"__q{i}__"] = (q.sem, q.n)

        def replay(items):
            def f(eng):
                seen = {}
                for it in items:
                    if isinstance(it, WaitOn):
                        for nm in it.names:
                            if nm is None:
                                continue
                            sem, n = K[nm]
                            if seen.get(sem.num, -1) >= n:
                                continue
                            seen[sem.num] = n
                            eng.wait_ge(sem, n)
                    else:
                        inst = getattr(eng, it.meth)(*it.args, **it.kwargs)
                        for sem, k in it.incs:
                            inst.then_inc(sem, k)
            return f

        with nc.Block() as block:
            block.tensor(replay(T.items))
            block.vector(replay(V.items))
            block.scalar(replay(A.items))
            block.gpsimd(replay(P.items))
            block.sync(replay(Y.items))

    return nc


_NC_CACHE = {}


def kernel(**inputs):
    x = np.asarray(inputs["L0_attn_hidden_states"], dtype=np.float32)
    wq = np.asarray(inputs["L0_attn_Wq"], dtype=np.float32)
    wk = np.asarray(inputs["L0_attn_Wk"], dtype=np.float32)
    wv = np.asarray(inputs["L0_attn_Wv"], dtype=np.float32)
    bv = np.asarray(inputs["L0_attn_bv"], dtype=np.float32)
    wo = np.asarray(inputs["L0_attn_Wo"], dtype=np.float32)
    wup = np.asarray(inputs["L0_ffn_W_up"], dtype=np.float32)
    gw = np.asarray(inputs["L0_ffn_gate_w"], dtype=np.float32)
    eb = np.asarray(inputs["L0_ffn_expert_bias"], dtype=np.float32)
    la = np.asarray(inputs["L0_ffn_lora_A"], dtype=np.float32)
    lb = np.asarray(inputs["L0_ffn_lora_B"], dtype=np.float32)

    c = np.ascontiguousarray
    wqT = c(wq.reshape(NQ, 128, NC, 128).transpose(0, 3, 2, 1))
    wkT = c(wk.reshape(NKV, 128, NC, 128).transpose(0, 3, 2, 1))
    wvR = c(wv.reshape(NKV * 128, NC, 128).transpose(1, 2, 0))
    woT = c(wo.reshape(NC, 128, NC, 128).transpose(0, 3, 2, 1))
    wupT = c(wup.reshape(NC, 128, NC, 128).transpose(0, 3, 2, 1))
    gateT = c(gw.T.reshape(NC, 128, 4))
    ebias = c(eb.reshape(4, 1))
    aT = c(la.reshape(32, D).T.reshape(NC, 128, 32))
    bflat = c(lb.transpose(0, 2, 1).reshape(32, D))
    bvc = c(bv.reshape(NKV, 128).T)
    xT = [c(x[b].T) for b in range(2)]

    in_maps = []
    for core in range(8):
        b, s = core // 4, core % 4
        in_maps.append({
            "xTb": xT[b],
            "xTmy": c(xT[b][:, MY * s:MY * (s + 1)]),
            "wqT": wqT, "wkT": wkT, "wvR": wvR, "woT": woT, "wupT": wupT,
            "gateT": gateT, "ebias": ebias, "aT": aT, "bflat": bflat,
            "bvc": bvc,
        })

    if "nc" not in _NC_CACHE:
        _NC_CACHE["nc"] = build_nc()
    nc = _NC_CACHE["nc"]
    res = run_bass_kernel_spmd(nc, in_maps, core_ids=list(range(8)))

    out = np.empty((2, S, D), np.float32)
    for core in range(8):
        b, s = core // 4, core % 4
        out[b, MY * s:MY * (s + 1), :] = res.results[core]["outT"].T
    return out


# version 25
# speedup vs baseline: 1.1581x; 1.0963x over previous
import numpy as np
import concourse.bass as bass
import concourse.mybir as mybir
from concourse.bass_utils import run_bass_kernel_spmd

F32 = mybir.dt.float32
BF16 = mybir.dt.bfloat16
AF = mybir.ActivationFunctionType
ALU = mybir.AluOpType
AX = mybir.AxisListType

D = 4096
S = 4096
MY = 1024
NQ, NKV, DH = 32, 8, 128
NC = D // 128
QSCALE = DH ** -0.5
EPS = 1e-5
P0T = 256
NT0 = S // P0T
PRT = 256
NTP = MY // PRT
SB = 512


class Item:
    __slots__ = ("meth", "args", "kwargs", "incs")

    def __init__(self, meth, args, kwargs):
        self.meth, self.args, self.kwargs = meth, args, kwargs
        self.incs = []

    def then_inc(self, sem, k):
        self.incs.append((sem, k))
        return self


class WaitOn:
    __slots__ = ("names",)

    def __init__(self, names):
        self.names = names


class Stream:
    def __init__(self):
        self.items = []

    def __getattr__(self, meth):
        if meth.startswith("__"):
            raise AttributeError(meth)

        def f(*a, **kw):
            it = Item(meth, a, kw)
            self.items.append(it)
            return it
        return f

    def wait(self, *names):
        if names:
            self.items.append(WaitOn(names))


class Ctr:
    def __init__(self, sem, step=1):
        self.sem, self.step, self.n = sem, step, 0

    def tick(self, item):
        item.then_inc(self.sem, self.step)
        self.n += self.step
        return self.n


def build_nc():
    nc = bass.Bass("TRN2", target_bir_lowering=False)

    xTb = nc.dram_tensor("xTb", [D, S], F32, kind="ExternalInput").ap()
    xTmy = nc.dram_tensor("xTmy", [D, MY], F32, kind="ExternalInput").ap()
    wqT = nc.dram_tensor("wqT", [NQ, 128, NC, 128], F32, kind="ExternalInput").ap()
    wkT = nc.dram_tensor("wkT", [NKV, 128, NC, 128], F32, kind="ExternalInput").ap()
    wvR = nc.dram_tensor("wvR", [NC, 128, NKV * 128], F32, kind="ExternalInput").ap()
    woT = nc.dram_tensor("woT", [NC, 128, NC, 128], F32, kind="ExternalInput").ap()
    wupT = nc.dram_tensor("wupT", [NC, 128, NC, 128], F32, kind="ExternalInput").ap()
    gateT = nc.dram_tensor("gateT", [NC, 128, 4], F32, kind="ExternalInput").ap()
    ebias = nc.dram_tensor("ebias", [4, 1], F32, kind="ExternalInput").ap()
    aT = nc.dram_tensor("aT", [NC, 128, 32], F32, kind="ExternalInput").ap()
    bflat = nc.dram_tensor("bflat", [32, D], F32, kind="ExternalInput").ap()
    bvc = nc.dram_tensor("bvc", [128, NKV], F32, kind="ExternalInput").ap()
    outT = nc.dram_tensor("outT", [D, MY], F32, kind="ExternalOutput").ap()

    k_d = nc.dram_tensor("k_d", [NKV, 128, S], BF16, kind="Internal").ap()
    xnst = nc.dram_tensor("xnst", [128, NC, MY], BF16, kind="Internal").ap()
    v_d = nc.dram_tensor("v_d", [NKV, S // 128, 128, 128], BF16, kind="Internal").ap()

    from contextlib import ExitStack
    with ExitStack() as top:
        ones_col = top.enter_context(nc.sbuf_tensor("ones_col", [128, 1], BF16))
        ones_row = top.enter_context(nc.sbuf_tensor("ones_row", [1, 128], F32))
        cst_eps = top.enter_context(nc.sbuf_tensor("cst_eps", [1, 1], F32))
        eb_sb = top.enter_context(nc.sbuf_tensor("eb_sb", [4, 1], F32))
        bvc_sb = top.enter_context(nc.sbuf_tensor("bvc_sb", [128, NKV], F32))
        idn = top.enter_context(nc.sbuf_tensor("idn", [128, 128], F32))
        e4 = top.enter_context(nc.sbuf_tensor("e4", [4, 32], BF16))
        gate_sb = top.enter_context(nc.sbuf_tensor("gate_sb", [128, NC, 4], BF16))
        m_row = top.enter_context(nc.sbuf_tensor("m_row", [1, P0T], F32))
        v_row = top.enter_context(nc.sbuf_tensor("v_row", [1, P0T], F32))
        r_row = top.enter_context(nc.sbuf_tensor("r_row", [1, P0T], F32))

        pb = [top.enter_context(nc.psum_tensor(f"pb{i}", [128, 512], F32))
              for i in range(8)]

        s0 = ExitStack()
        x0 = [s0.enter_context(nc.sbuf_tensor(f"x0_{i}", [128, NC, P0T], BF16))
              for i in range(2)]
        wk_sb = s0.enter_context(nc.sbuf_tensor("wk_sb", [128, NKV, NC, 128], BF16))
        wv_sb = s0.enter_context(nc.sbuf_tensor("wv_sb", [128, NC, NKV * 128], BF16))
        xn0 = [s0.enter_context(nc.sbuf_tensor(f"xn0_{i}", [128, NC, P0T], BF16))
               for i in range(2)]
        x20 = s0.enter_context(nc.sbuf_tensor("x20", [128, 2, P0T], BF16))
        xf0 = s0.enter_context(nc.sbuf_tensor("xf0", [128, 2, P0T], F32))
        kt0 = [s0.enter_context(nc.sbuf_tensor(f"kt0_{i}", [128, P0T], BF16))
               for i in range(2)]
        vt0 = [s0.enter_context(nc.sbuf_tensor(f"vt0_{i}", [128, NKV * 128], BF16))
               for i in range(2)]
        s0.close()

        xn_my = top.enter_context(nc.sbuf_tensor("xn_my", [128, NC, MY], BF16))
        ctx_sb = top.enter_context(nc.sbuf_tensor("ctx_sb", [128, NC, MY], BF16))

        s1 = ExitStack()
        wq_sb = [s1.enter_context(nc.sbuf_tensor(f"wq_sb{i}", [128, NC, 128], BF16))
                 for i in range(2)]
        kh_sb = [s1.enter_context(nc.sbuf_tensor(f"kh_sb{i}", [128, S], BF16))
                 for i in range(2)]
        vh_sb = [s1.enter_context(nc.sbuf_tensor(f"vh_sb{i}", [128, S // 128, 128], BF16))
                 for i in range(2)]
        q_sb = s1.enter_context(nc.sbuf_tensor("q_sb", [128, MY], BF16))
        expp = s1.enter_context(nc.sbuf_tensor("expp", [128, 4, SB], BF16))
        dpair = s1.enter_context(nc.sbuf_tensor("dpair", [128, 10, SB], BF16))
        recip_b = s1.enter_context(nc.sbuf_tensor("recip_b", [128, SB], F32))
        ct_f = s1.enter_context(nc.sbuf_tensor("ct_f", [128, SB], F32))
        den_row = s1.enter_context(nc.sbuf_tensor("den_row", [1, SB], F32))
        s1.close()

        s2 = ExitStack()
        wo_sb = [s2.enter_context(nc.sbuf_tensor(f"wo_sb{i}", [128, NC, 128], BF16))
                 for i in range(2)]
        wu_sb = [s2.enter_context(nc.sbuf_tensor(f"wu_sb{i}", [128, NC, 128], BF16))
                 for i in range(2)]
        xres = [s2.enter_context(nc.sbuf_tensor(f"xres{i}", [128, MY], F32))
                for i in range(2)]
        at_sb = s2.enter_context(nc.sbuf_tensor("at_sb", [128, NC, 32], BF16))
        bf_sb = s2.enter_context(nc.sbuf_tensor("bf_sb", [32, D], BF16))
        lg_sb = s2.enter_context(nc.sbuf_tensor("lg_sb", [4, MY], F32))
        lgT_sb = s2.enter_context(nc.sbuf_tensor("lgT_sb", [128, MY // 128, 4], F32))
        mx_sb = s2.enter_context(nc.sbuf_tensor("mx_sb", [128, MY // 128, 1], F32))
        oh_sb = s2.enter_context(nc.sbuf_tensor("oh_sb", [128, MY // 128, 4], F32))
        ohT_sb = s2.enter_context(nc.sbuf_tensor("ohT_sb", [4, MY], BF16))
        m32_sb = s2.enter_context(nc.sbuf_tensor("m32_sb", [32, MY], BF16))
        zsel_sb = s2.enter_context(nc.sbuf_tensor("zsel_sb", [32, MY], BF16))
        g_sb = s2.enter_context(nc.sbuf_tensor("g_sb", [128, SB], F32))
        t_sb = s2.enter_context(nc.sbuf_tensor("t_sb", [128, SB], F32))
        ot_sb = [s2.enter_context(nc.sbuf_tensor(f"ot_sb{i}", [128, MY], F32))
                 for i in range(2)]
        s2.close()

        sPE = Ctr(top.enter_context(nc.semaphore("sPE")))
        sDVE = Ctr(top.enter_context(nc.semaphore("sDVE")))
        sACT = Ctr(top.enter_context(nc.semaphore("sACT")))
        sPOOL = Ctr(top.enter_context(nc.semaphore("sPOOL")))
        def dsem(name):
            return Ctr(top.enter_context(nc.semaphore(name)), 16)

        qW0 = dsem("qW0")
        qX = dsem("qX")
        qK = [dsem("qK0"), dsem("qK1")]
        qV = [dsem("qV0"), dsem("qV1")]
        qQ = [dsem("qQ0"), dsem("qQ1")]
        qWo = [dsem("qWo0"), dsem("qWo1")]
        qWu = [dsem("qWu0"), dsem("qWu1")]
        qWb = dsem("qWb")
        qR = [dsem("qR0"), dsem("qR1")]
        qSK = [dsem("qSK0"), dsem("qSK1")]
        qSV = [dsem("qSV0"), dsem("qSV1")]
        qSO = [dsem("qSO0"), dsem("qSO1")]
        qXS = [dsem("qXS0"), dsem("qXS1")]
        qXM = dsem("qXM")
        qWK = [dsem(f"qWK{h}") for h in range(NKV)]
        qWV = [dsem("qWVa"), dsem("qWVb")]
        all_q = ([qW0, qX, qWb, qXM] + qK + qV + qQ + qWo + qWu + qR
                 + qSK + qSV + qSO + qXS + qWK + qWV)

        K = {}

        def mark(name, ctr, item):
            K[name] = (ctr.sem, ctr.tick(item))

        T, V, A, P, Y = Stream(), Stream(), Stream(), Stream(), Stream()

        sPOOL.tick(P.memset(ones_col[:], 1.0))
        sPOOL.tick(P.memset(ones_row[:], 1.0))
        sPOOL.tick(P.memset(cst_eps[:], EPS))
        mark("idn0", sPOOL, P.memset(idn[:], 0.0))
        P.wait("idn0")
        mark("idn", sPOOL,
             P.affine_select(out=idn[:], in_=idn[:], compare_op=ALU.not_equal,
                             fill=1.0, base=0, pattern=[[-1, 128]],
                             channel_multiplier=1))
        mark("e40", sPOOL, P.memset(e4[:], 1.0))
        P.wait("e40")
        mark("e41", sPOOL, P.affine_select(
            e4[:], e4[:], pattern=[[1, 32]], channel_multiplier=-8,
            base=0, compare_op=ALU.is_ge, fill=0.0))
        P.wait("e41")
        mark("consts", sPOOL, P.affine_select(
            e4[:], e4[:], pattern=[[-1, 32]], channel_multiplier=8,
            base=7, compare_op=ALU.is_ge, fill=0.0))
        K["e4done"] = K["consts"]
        qW0.tick(P.dma_start(eb_sb[:], ebias))
        qW0.tick(P.dma_start(bvc_sb[:], bvc))
        mark("gws", qW0, P.dma_start(gate_sb[:], gateT.rearrange("c p e -> p c e")))

        chunk_par = [0]

        def ln_tile(tag, xsrc, xn_dst, n_tok, first_waits, prev=None,
                    norm_first_waits=(), bcp=(2, 3)):
            ps_sum = pb[0][0:1, 0:n_tok]
            ps_sq = pb[1][0:1, 0:n_tok]
            ps_mb = pb[bcp[0]][:, 0:n_tok]
            ps_rb = pb[bcp[1]][:, 0:n_tok]
            for c in range(NC):
                j = chunk_par[0]
                chunk_par[0] += 1
                sl = j % 2
                V.wait(f"{tag}src" if c == 0 else None) if c == 0 else None
                if c == 0:
                    V.wait(f"{tag}src")
                if j >= 2:
                    V.wait(f"lnB_{j-2}")
                mark(f"lnq_{j}", sDVE,
                     V.tensor_mul(x20[:, sl, 0:n_tok], xsrc[:, c, 0:n_tok],
                                  xsrc[:, c, 0:n_tok]))
                T.wait(f"lnq_{j}")
                if c == 0:
                    T.wait(f"{tag}src", *first_waits)
                T.matmul(ps_sum, ones_col[:], xsrc[:, c, 0:n_tok],
                         start=(c == 0), stop=(c == NC - 1))
                i2 = T.matmul(ps_sq, ones_col[:], x20[:, sl, 0:n_tok],
                              start=(c == 0), stop=(c == NC - 1))
                mark(f"lnB_{j}", sPE, i2)
            jl = chunk_par[0] - 1
            V.wait(f"lnB_{jl}")
            if prev is not None:
                V.wait(f"{prev}bm", f"{prev}bc", f"{prev}ln")
            mark(f"{tag}m", sDVE,
                 V.tensor_scalar_mul(m_row[:, 0:n_tok], ps_sum, 1.0 / D))
            mark(f"{tag}v0", sDVE,
                 V.tensor_scalar_mul(v_row[:, 0:n_tok], ps_sq, 1.0 / D))
            V.wait(f"{tag}m", f"{tag}v0")
            mark(f"{tag}m2", sDVE,
                 V.tensor_mul(r_row[:, 0:n_tok], m_row[:, 0:n_tok],
                              m_row[:, 0:n_tok]))
            V.wait(f"{tag}m2")
            mark(f"{tag}var", sDVE,
                 V.tensor_sub(v_row[:, 0:n_tok], v_row[:, 0:n_tok],
                              r_row[:, 0:n_tok]))
            A.wait(f"{tag}var")
            if prev is not None:
                A.wait(f"{prev}bc")
            mark(f"{tag}ln", sACT,
                 A.activation(r_row[:, 0:n_tok], v_row[:, 0:n_tok], AF.Ln,
                              bias=cst_eps[:, :]))
            A.wait(f"{tag}ln")
            mark(f"{tag}rstd", sACT,
                 A.activation(r_row[:, 0:n_tok], r_row[:, 0:n_tok], AF.Exp,
                              scale=-0.5))
            T.wait(f"{tag}m")
            mark(f"{tag}bm", sPE,
                 T.matmul(ps_mb, ones_row[:], m_row[:, 0:n_tok],
                          start=True, stop=True))
            T.wait(f"{tag}rstd")
            mark(f"{tag}bc", sPE,
                 T.matmul(ps_rb, ones_row[:], r_row[:, 0:n_tok],
                          start=True, stop=True))
            for c in range(NC):
                sl = c % 2
                V.wait(f"{tag}bc")
                if c == 0:
                    V.wait(*norm_first_waits)
                mark(f"{tag}sub{c}", sDVE,
                     V.tensor_sub(xf0[:, sl, 0:n_tok], xsrc[:, c, 0:n_tok], ps_mb))
                V.wait(f"{tag}sub{c}")
                mark(f"{tag}xn{c}", sDVE,
                     V.tensor_tensor(xn_dst(c), xf0[:, sl, 0:n_tok], ps_rb,
                                     ALU.mult))


        def kv_emit(tt, xtag):
            for h in range(NKV):
                hh = tt * NKV + h
                if tt == 0:
                    T.wait(f"wkh_{h}")
                if hh > 0:
                    T.wait(f"kev_{hh-1}")
                if h == 0 and tt > 0:
                    T.wait(f"vev_{tt*2-1}_1")
                for c in range(NC):
                    if h == 0:
                        T.wait(f"{xtag}xn{c}")
                    i = T.matmul(pb[4][:, 0:P0T], wk_sb[:, h, c, :],
                                 xn0[tt % 2][:, c, :],
                                 start=(c == 0), stop=(c == NC - 1))
                mark(f"kmm_{hh}", sPE, i)
                A.wait(f"kmm_{hh}")
                if hh >= 2:
                    A.wait(f"kst_{hh-2}")
                mark(f"kev_{hh}", sACT,
                     A.activation(kt0[h % 2][:], pb[4][:, 0:P0T], AF.Copy))
                A.wait(f"kev_{hh}")
                mark(f"kst_{hh}", qSK[hh % 2],
                     A.dma_start(k_d[h, :, tt * P0T:(tt + 1) * P0T], kt0[h % 2][:]))
            for tc in range(P0T // 128):
                tcg = tt * (P0T // 128) + tc
                if tcg > 0:
                    T.wait(f"vev_{tcg-1}_0", f"vev_{tcg-1}_1")
                T.wait(f"kev_{tt*NKV+NKV-1}")
                for half in range(2):
                    ps = pb[5 if half == 0 else 4][:, 0:512]
                    if tt == 0 and tc == 0:
                        T.wait("wva" if half == 0 else "w0")
                    for c in range(NC):
                        i = T.matmul(ps, xn0[tt % 2][:, c, tc * 128:(tc + 1) * 128],
                                     wv_sb[:, c, half * 512:(half + 1) * 512],
                                     start=(c == 0), stop=(c == NC - 1))
                    mark(f"vmm_{tcg}_{half}", sPE, i)
                for half in range(2):
                    A.wait(f"vmm_{tcg}_{half}")
                    if tcg >= 2:
                        A.wait(f"vst_{tcg-2}")
                    mark(f"vev_{tcg}_{half}", sACT,
                         A.activation(vt0[tc % 2][:, half * 512:(half + 1) * 512],
                                      pb[5 if half == 0 else 4][:, 0:512], AF.Copy))
                A.wait(f"vev_{tcg}_0", f"vev_{tcg}_1")
                for h in range(NKV):
                    i = A.dma_start(v_d[h, tcg], vt0[tc % 2][:, h * 128:(h + 1) * 128])
                    if h < NKV - 1:
                        qSV[tcg % 2].tick(i)
                mark(f"vst_{tcg}", qSV[tcg % 2], i)

        seq = []
        bi = 0
        for i4 in range(4):
            seq += [("b", bi), ("b", bi + 1), ("b", bi + 2), ("b", bi + 3)]
            seq.append(("p", i4))
            bi += 4
        kvq = []
        done_kv = 0
        for sidx, (kind, ti) in enumerate(seq):
            tg = f"t{sidx}"
            if sidx >= 2:
                P.wait(f"t{sidx-2}xn{NC-1}", f"lnB_{(sidx-1)*NC-1}")
                if seq[sidx - 2][0] == "p":
                    P.wait(f"xst_{seq[sidx-2][1]}")
            if sidx >= 1:
                P.wait(f"lnq_{(sidx-1)*NC}")
            if kind == "b":
                src_ap = xTb.rearrange("(c p) t -> p c t", p=128)[
                    :, :, ti * P0T:(ti + 1) * P0T]
            else:
                src_ap = xTmy.rearrange("(c p) t -> p c t", p=128)[
                    :, :, ti * P0T:(ti + 1) * P0T]
            mark(f"{tg}src", qX, P.dma_start(x0[sidx % 2][:], src_ap))
            if sidx == 1:
                for h in range(2):
                    mark(f"wkh_{h}", qWK[h],
                         P.dma_start(wk_sb[:, h], wkT[h]))
            elif sidx == 2:
                mark("wva", qWV[0],
                     P.dma_start(wv_sb[:, :, 0:512],
                                 wvR.rearrange("c p v -> p c v")[:, :, 0:512]))
                for h in range(2, 4):
                    mark(f"wkh_{h}", qWK[h],
                         P.dma_start(wk_sb[:, h], wkT[h]))
            elif sidx == 3:
                mark("w0", qWV[1],
                     P.dma_start(wv_sb[:, :, 512:1024],
                                 wvR.rearrange("c p v -> p c v")[:, :, 512:1024]))
                for h in range(4, NKV):
                    mark(f"wkh_{h}", qWK[h],
                         P.dma_start(wk_sb[:, h], wkT[h]))
            fw = ["consts"] if sidx == 0 else [f"t{sidx-1}m", f"t{sidx-1}v0"]
            if sidx >= 2:
                fw.append(f"t{sidx-2}xn{NC-1}")
            if kind == "b":
                nfw = [f"vmm_{(ti-2)*2+1}_1"] if ti >= 2 else ()
                xnd = (lambda c, sl=ti % 2: xn0[sl][:, c, 0:P0T])
            else:
                nfw = ()
                xnd = (lambda c, sl=sidx % 2: x0[sl][:, c, 0:P0T])
            ln_tile(tg, x0[sidx % 2], xnd, P0T, fw,
                    prev=(f"t{sidx-1}" if sidx > 0 else None),
                    norm_first_waits=nfw,
                    bcp=((2, 3) if sidx % 2 == 0 else (6, 7)))
            if kind == "b":
                kvq.append((ti, tg, sidx))
            else:
                Y.wait(f"{tg}xn{NC-1}")
                mark(f"xst_{ti}", qXS[ti % 2],
                     Y.dma_start(xnst[:, :, ti * P0T:(ti + 1) * P0T],
                                 x0[sidx % 2][:]))
            if kvq and kvq[0][2] < sidx:
                bti, btg, _ = kvq.pop(0)
                kv_emit(bti, btg)
        while kvq:
            bti, btg, _ = kvq.pop(0)
            kv_emit(bti, btg)

        k_done = (f"kst_{NT0*NKV-2}", f"kst_{NT0*NKV-1}")
        v_done = (f"vst_{NT0*(P0T//128)-2}", f"vst_{NT0*(P0T//128)-1}")
        p0_pe_done = f"vmm_{NT0*(P0T//128)-1}_1"

        Y.wait("xst_0", "xst_1", "xst_2", "xst_3",
               f"t{len(seq)-1}xn{NC-1}", f"lnB_{len(seq)*NC-1}",
               *k_done, *v_done, p0_pe_done)
        mark("xnmy", qXM, Y.dma_start(xn_my[:], xnst))

        pro_done = "xnmy"
        pro_rows = [f"t{NT0+NTP-1}m", f"t{NT0+NTP-1}v0"]

        for h in range(NKV):
            P.wait(*k_done, *v_done)
            if h >= 2:
                P.wait(f"qk_{4*(h-2)+3}_1_31")
            mark(f"kh_{h}", qK[h % 2], P.dma_start(kh_sb[h % 2][:], k_d[h]))
            if h >= 2:
                P.wait(f"pvl_{4*(h-2)+3}_1")
            mark(f"vh_{h}", qV[h % 2],
                 P.dma_start(vh_sb[h % 2][:],
                             v_d[h].rearrange("c p d -> p c d")))
            for g in range(4):
                qh = h * 4 + g
                if qh < 2:
                    P.wait(pro_done, *pro_rows, *k_done, *v_done, p0_pe_done)
                else:
                    P.wait(f"qmm_{qh-2}_1")
                mark(f"wq_{qh}", qQ[qh % 2], P.dma_start(wq_sb[qh % 2][:], wqT[qh]))

        gchunk = [0]

        def qproj_emit(qh, interleave=None):
            if interleave is None:
                T.wait(f"wq_{qh}")
                if qh == 0:
                    T.wait(pro_done, *pro_rows)
                for sb2 in range(2):
                    for c in range(NC):
                        i = T.matmul(pb[sb2][:, 0:512], wq_sb[qh % 2][:, c, :],
                                     xn_my[:, c, sb2 * 512:(sb2 + 1) * 512],
                                     start=(c == 0), stop=(c == NC - 1))
                    mark(f"qmm_{qh}_{sb2}", sPE, i)
                return
            it = interleave
            sb2, c = it // NC, it % NC
            if it == 0:
                T.wait(f"wq_{qh}")
                T.wait(f"qev_{qh-1}_0", f"qev_{qh-1}_1")
            i = T.matmul(pb[sb2][:, 0:512], wq_sb[qh % 2][:, c, :],
                         xn_my[:, c, sb2 * 512:(sb2 + 1) * 512],
                         start=(c == 0), stop=(c == NC - 1))
            if c == NC - 1:
                mark(f"qmm_{qh}_{sb2}", sPE, i)

        def qev_emit(qh):
            for sb2 in range(2):
                V.wait(f"qmm_{qh}_{sb2}")
                if qh > 0:
                    V.wait(f"qk_{qh-1}_1_31")
                mark(f"qev_{qh}_{sb2}", sDVE,
                     V.tensor_scalar(q_sb[:, sb2 * 512:(sb2 + 1) * 512],
                                     pb[sb2][:, 0:512], QSCALE, None,
                                     op0=ALU.mult))

        qproj_emit(0)
        qev_emit(0)
        pending_bc = [None]
        pending_vt = [None]
        for kvh in range(NKV):
            for g in range(4):
                qh = kvh * 4 + g
                for sb in range(2):
                    jbase = gchunk[0]
                    T.wait(f"qev_{qh}_{sb}", f"kh_{kvh}")
                    SCB = (2, 3, 7)
                    for c in range(3):
                        j = jbase + c
                        if j >= 3:
                            T.wait(f"exp_{j-3}")
                        mark(f"qk_{qh}_{sb}_{c}", sPE,
                             T.matmul(pb[SCB[j % 3]][:, 0:SB],
                                      kh_sb[kvh % 2][:, c * 128:(c + 1) * 128],
                                      q_sb[:, sb * 512:(sb + 1) * 512],
                                      start=True, stop=True))
                    for c in range(NC):
                        j = jbase + c
                        A.wait(f"qk_{qh}_{sb}_{c}")
                        if j >= 4:
                            A.wait(f"pr_{(j-4)//2}")
                        mark(f"exp_{j}", sACT,
                             A.activation(expp[:, j % 4, :], pb[SCB[j % 3]][:, 0:SB],
                                          AF.Exp))
                        T.wait(f"exp_{j}")
                        if c == 0:
                            T.wait(f"vh_{kvh}")
                            if qh > 0 or sb > 0:
                                pq, psb = (qh, 0) if sb else (qh - 1, 1)
                                T.wait(f"ctxm_{pq}_{psb}")
                        i = T.matmul(pb[4][:, 0:SB], vh_sb[kvh % 2][:, c, :],
                                     expp[:, j % 4, :],
                                     start=(c == 0), stop=(c == NC - 1))
                        if c == NC - 1:
                            mark(f"pvl_{qh}_{sb}", sPE, i)
                        if qh < NQ - 1:
                            qproj_emit(qh + 1, interleave=sb * NC + c)
                        if c % 2 == 1:
                            V.wait(f"exp_{j}")
                            if qh == 0 and sb == 0 and c == 1:
                                V.wait(f"vev_{NT0*(P0T//128)-1}_1")
                            if j // 2 >= 2:
                                V.wait(f"qd_{(j-4)//4}")
                            mark(f"pr_{j//2}", sDVE,
                                 V.tensor_add(dpair[:, 8 + ((c - 1) // 2) % 2, :],
                                              expp[:, (j - 1) % 4, :],
                                              expp[:, j % 4, :]))
                        if c % 4 == 3:
                            V.wait(f"pr_{j//2}", f"pr_{(j-2)//2}")
                            if c == 3 and (qh > 0 or sb > 0):
                                V.wait(f"den_{jbase-1}")
                            mark(f"qd_{j//4}", sDVE,
                                 V.tensor_add(dpair[:, (c - 3) // 4, :],
                                              dpair[:, 8, :],
                                              dpair[:, 9, :]))
                        if c == 5 and pending_vt[0] is not None:
                            V.wait("gws")
                            pq, psb, pkv = pending_vt[0]
                            V.wait(f"bc_{pq}_{psb}")
                            mark(f"rc_{pq}_{psb}", sDVE,
                                 V.tensor_copy(recip_b[:], pb[6][:, 0:SB]))
                            V.wait(f"rc_{pq}_{psb}")
                            mark(f"ctxn_{pq}_{psb}", sDVE,
                                 V.tensor_tensor(ct_f[:], ct_f[:], recip_b[:],
                                                 ALU.mult))
                            V.wait(f"ctxn_{pq}_{psb}")
                            mark(f"ctxw_{pq}_{psb}", sDVE,
                                 V.tensor_scalar(
                                     ctx_sb[:, pq, psb * 512:(psb + 1) * 512],
                                     ct_f[:], bvc_sb[:, pkv:pkv + 1], None,
                                     op0=ALU.add))
                            pending_vt[0] = None
                        if c == 3 and pending_bc[0] is not None:
                            pq, psb = pending_bc[0]
                            T.wait(f"dre_{pq}_{psb}")
                            if pq > 0 or psb > 0:
                                p2q, p2sb = (pq, 0) if psb else (pq - 1, 1)
                                T.wait(f"rc_{p2q}_{p2sb}")
                            mark(f"bc_{pq}_{psb}", sPE,
                                 T.matmul(pb[6][:, 0:SB], ones_row[:], den_row[:],
                                          start=True, stop=True))
                            pending_bc[0] = None
                        if c + 3 < NC:
                            j2 = jbase + c + 3
                            T.wait(f"exp_{j2-3}")
                            mark(f"qk_{qh}_{sb}_{c+3}", sPE,
                                 T.matmul(pb[SCB[j2 % 3]][:, 0:SB],
                                          kh_sb[kvh % 2][:, (c + 2 + 1) * 128:(c + 4) * 128],
                                          q_sb[:, sb * 512:(sb + 1) * 512],
                                          start=True, stop=True))
                    if qh > 0 or sb > 0:
                        pq, psb = (qh, 0) if sb else (qh - 1, 1)
                        T.wait(f"denr_{pq}_{psb}")
                    for i2 in range(NC // 4):
                        T.wait(f"qd_{(jbase + 4*i2 + 3)//4}")
                        di = T.matmul(pb[5][0:1, 0:SB], ones_col[:],
                                      dpair[:, i2, :],
                                      start=(i2 == 0), stop=(i2 == NC // 4 - 1))
                    mark(f"den_{jbase+NC-1}", sPE, di)
                    gchunk[0] += NC
                    V.wait(f"pvl_{qh}_{sb}")
                    if qh > 0 or sb > 0:
                        pq, psb = (qh, 0) if sb else (qh - 1, 1)
                        V.wait(f"ctxw_{pq}_{psb}")
                    mark(f"ctxm_{qh}_{sb}", sDVE,
                         V.tensor_copy(ct_f[:], pb[4][:, 0:SB]))
                    V.wait(f"den_{jbase+NC-1}")
                    if qh > 0 or sb > 0:
                        pq, psb = (qh, 0) if sb else (qh - 1, 1)
                        V.wait(f"bc_{pq}_{psb}")
                        V.wait(f"dre_{pq}_{psb}")
                    mark(f"denr_{qh}_{sb}", sDVE,
                         V.tensor_copy(den_row[:], pb[5][0:1, 0:SB]))
                    A.wait(f"denr_{qh}_{sb}")
                    mark(f"dln_{qh}_{sb}", sACT,
                         A.activation(den_row[:], den_row[:], AF.Ln))
                    A.wait(f"dln_{qh}_{sb}")
                    mark(f"dre_{qh}_{sb}", sACT,
                         A.activation(den_row[:], den_row[:], AF.Exp, scale=-1.0))
                    pending_bc[0] = (qh, sb)
                    pending_vt[0] = (qh, sb, kvh)
                if qh < NQ - 1:
                    qev_emit(qh + 1)

        pq, psb = pending_bc[0]
        T.wait(f"dre_{pq}_{psb}")
        p2q, p2sb = (pq, 0) if psb else (pq - 1, 1)
        T.wait(f"rc_{p2q}_{p2sb}")
        mark(f"bc_{pq}_{psb}", sPE,
             T.matmul(pb[6][:, 0:SB], ones_row[:], den_row[:],
                      start=True, stop=True))
        pending_bc[0] = None
        pq, psb, pkv = pending_vt[0]
        V.wait("gws")
        V.wait(f"bc_{pq}_{psb}")
        mark(f"rc_{pq}_{psb}", sDVE, V.tensor_copy(recip_b[:], pb[6][:, 0:SB]))
        V.wait(f"rc_{pq}_{psb}")
        mark(f"ctxn_{pq}_{psb}", sDVE,
             V.tensor_tensor(ct_f[:], ct_f[:], recip_b[:], ALU.mult))
        V.wait(f"ctxn_{pq}_{psb}")
        mark(f"ctxw_{pq}_{psb}", sDVE,
             V.tensor_scalar(ctx_sb[:, pq, psb * 512:(psb + 1) * 512],
                             ct_f[:], bvc_sb[:, pkv:pkv + 1], None, op0=ALU.add))
        pending_vt[0] = None

        att_done = f"ctxw_{NQ-1}_1"
        att_pe_done = f"den_{gchunk[0]-1}"

        P.wait(att_done, att_pe_done)
        qWb.tick(P.dma_start(at_sb[:], aT.rearrange("c p e -> p c e")))
        mark("w1b", qWb, P.dma_start(bf_sb[:], bflat))
        for oc in range(NC):
            if oc >= 2:
                P.wait(f"womm_{oc-2}_1")
            mark(f"wo_{oc}", qWo[oc % 2], P.dma_start(wo_sb[oc % 2][:], woT[oc]))
            if oc >= 2:
                P.wait(f"ffnm_{oc-2}_1")
            mark(f"wu_{oc}", qWu[oc % 2], P.dma_start(wu_sb[oc % 2][:], wupT[oc]))
            if oc < 2:
                Y.wait(att_done)
            else:
                Y.wait(f"ow_{oc-2}_1")
            mark(f"xr_{oc}", qR[oc % 2],
                 Y.dma_start(xres[oc % 2][:], xTmy[oc * 128:(oc + 1) * 128, :]))

        def womm_emit(oc):
            T.wait(f"wo_{oc}")
            if oc >= 2:
                T.wait(f"tm_{oc-2}_0", f"tm_{oc-2}_1")
            pp = 4 if oc % 2 == 0 else 6
            for sb in range(2):
                for c in range(NC):
                    i = T.matmul(pb[pp + sb][:, 0:512], wo_sb[oc % 2][:, c, :],
                                 ctx_sb[:, c, sb * 512:(sb + 1) * 512],
                                 start=(c == 0), stop=(c == NC - 1))
                mark(f"womm_{oc}_{sb}", sPE, i)

        def ffnm_emit(oc):
            T.wait(f"wu_{oc}", "zs_0", "zs_1")
            if oc >= 2:
                T.wait(f"gev_{oc-2}_1")
            elif oc == 0:
                T.wait("lgev_0", "lgev_1")
            qp = 0 if oc % 2 == 0 else 2
            for sb in range(2):
                for c in range(NC):
                    T.matmul(pb[qp + sb][:, 0:512], wu_sb[oc % 2][:, c, :],
                             xn_my[:, c, sb * 512:(sb + 1) * 512],
                             start=(c == 0), stop=False)
                mark(f"ffnm_{oc}_{sb}", sPE,
                     T.matmul(pb[qp + sb][:, 0:512],
                              bf_sb[:, oc * 128:(oc + 1) * 128],
                              zsel_sb[:, sb * 512:(sb + 1) * 512],
                              start=False, stop=True))

        def tail_emit(oc):
            pp = 4 if oc % 2 == 0 else 6
            qp = 0 if oc % 2 == 0 else 2
            for sb in range(2):
                A.wait(f"ffnm_{oc}_{sb}")
                if oc > 0 or sb > 0:
                    poc, psb = (oc, 0) if sb else (oc - 1, 1)
                    A.wait(f"tm_{poc}_{psb}")
                mark(f"gev_{oc}_{sb}", sACT,
                     A.activation(g_sb[:], pb[qp + sb][:, 0:512], AF.Silu))
                V.wait(f"womm_{oc}_{sb}", f"gev_{oc}_{sb}")
                if oc > 0 or sb > 0:
                    poc, psb = (oc, 0) if sb else (oc - 1, 1)
                    V.wait(f"ow_{poc}_{psb}")
                mark(f"tm_{oc}_{sb}", sDVE,
                     V.tensor_tensor(t_sb[:], pb[pp + sb][:, 0:512],
                                     g_sb[:], ALU.mult))
                V.wait(f"tm_{oc}_{sb}", f"xr_{oc}")
                if oc >= 2:
                    V.wait(f"ost_{oc-2}")
                mark(f"ow_{oc}_{sb}", sDVE,
                     V.tensor_add(ot_sb[oc % 2][:, sb * 512:(sb + 1) * 512],
                                  t_sb[:],
                                  xres[oc % 2][:, sb * 512:(sb + 1) * 512]))
            A.wait(f"ow_{oc}_0", f"ow_{oc}_1")
            mark(f"ost_{oc}", qSO[oc % 2],
                 A.dma_start(outT[oc * 128:(oc + 1) * 128, :], ot_sb[oc % 2][:]))

        T.wait(att_done, att_pe_done, "gws")
        for sb in range(2):
            T.wait(f"qev_{NQ-1}_{sb}")
            for c in range(NC):
                i = T.matmul(pb[sb][0:4, 0:512], gate_sb[:, c, :],
                             xn_my[:, c, sb * 512:(sb + 1) * 512],
                             start=(c == 0), stop=(c == NC - 1))
            mark(f"lgmm_{sb}", sPE, i)
            V.wait(f"lgmm_{sb}", att_done, "gws")
            mark(f"lgev_{sb}", sDVE,
                 V.tensor_scalar(lg_sb[:, sb * 512:(sb + 1) * 512],
                                 pb[sb][0:4, 0:512], eb_sb[:, 0:1], None,
                                 op0=ALU.add))
        womm_emit(0)
        womm_emit(1)
        for b in range(MY // 128):
            T.wait(f"lgev_{b // 4}", "idn")
            if b > 0:
                T.wait(f"lgtc_{b-1}")
            mark(f"lgt_{b}", sPE,
                 T.transpose(pb[2][:, 0:4], lg_sb[:, b * 128:(b + 1) * 128],
                             idn[0:4, 0:4]))
            V.wait(f"lgt_{b}")
            mark(f"lgtc_{b}", sDVE, V.tensor_copy(lgT_sb[:, b, :], pb[2][:, 0:4]))
        for b in range(MY // 128):
            V.wait(f"lgtc_{b}")
            mark(f"mx_{b}", sDVE,
                 V.reduce_max(mx_sb[:, b, :], lgT_sb[:, b, :], axis=AX.X))
            V.wait(f"mx_{b}")
            mark(f"oh_{b}", sDVE,
                 V.tensor_scalar(oh_sb[:, b, :], lgT_sb[:, b, :], mx_sb[:, b, 0:1],
                                 None, op0=ALU.is_ge))
        for b in range(MY // 128):
            T.wait(f"oh_{b}", f"lgtc_{MY//128-1}")
            if b > 0:
                T.wait(f"ohtc_{b-1}")
            mark(f"oht_{b}", sPE,
                 T.transpose(pb[2][0:4, 0:128], oh_sb[:, b, :], idn[:, :]))
            V.wait(f"oht_{b}")
            mark(f"ohtc_{b}", sDVE,
                 V.tensor_copy(ohT_sb[:, b * 128:(b + 1) * 128], pb[2][0:4, 0:128]))
        for sb in range(2):
            T.wait(f"ohtc_{MY//128-1}", "e4done")
            if sb > 0:
                T.wait("m32e_0")
            mark(f"m32m_{sb}", sPE,
                 T.matmul(pb[2][0:32, 0:512], e4[:],
                          ohT_sb[:, sb * 512:(sb + 1) * 512], start=True, stop=True))
            V.wait(f"m32m_{sb}")
            mark(f"m32e_{sb}", sDVE,
                 V.tensor_copy(m32_sb[:, sb * 512:(sb + 1) * 512],
                               pb[2][0:32, 0:512]))
        for sb in range(2):
            if sb > 0:
                T.wait("zs_0")
            else:
                T.wait("w1b")
            for c in range(NC):
                i = T.matmul(pb[3][0:32, 0:512], at_sb[:, c, :],
                             xn_my[:, c, sb * 512:(sb + 1) * 512],
                             start=(c == 0), stop=(c == NC - 1))
            mark(f"zmm_{sb}", sPE, i)
            V.wait(f"zmm_{sb}", f"m32e_{sb}")
            mark(f"zs_{sb}", sDVE,
                 V.tensor_tensor(zsel_sb[:, sb * 512:(sb + 1) * 512],
                                 pb[3][0:32, 0:512],
                                 m32_sb[:, sb * 512:(sb + 1) * 512], ALU.mult))
        ffnm_emit(0)
        tail_emit(0)
        ffnm_emit(1)
        tail_emit(1)
        for oc in range(2, NC):
            womm_emit(oc)
            ffnm_emit(oc)
            tail_emit(oc)

        Y.wait(f"ost_{NC-2}", f"ost_{NC-1}")
        Y.items.append(WaitOn(tuple(f"__q{i}__" for i in range(len(all_q)))))

        for i, q in enumerate(all_q):
            K[f"__q{i}__"] = (q.sem, q.n)

        def replay(items):
            def f(eng):
                seen = {}
                for it in items:
                    if isinstance(it, WaitOn):
                        for nm in it.names:
                            if nm is None:
                                continue
                            sem, n = K[nm]
                            if seen.get(sem.num, -1) >= n:
                                continue
                            seen[sem.num] = n
                            eng.wait_ge(sem, n)
                    else:
                        inst = getattr(eng, it.meth)(*it.args, **it.kwargs)
                        for sem, k in it.incs:
                            inst.then_inc(sem, k)
            return f

        with nc.Block() as block:
            block.tensor(replay(T.items))
            block.vector(replay(V.items))
            block.scalar(replay(A.items))
            block.gpsimd(replay(P.items))
            block.sync(replay(Y.items))

    return nc


_NC_CACHE = {}


def kernel(**inputs):
    x = np.asarray(inputs["L0_attn_hidden_states"], dtype=np.float32)
    wq = np.asarray(inputs["L0_attn_Wq"], dtype=np.float32)
    wk = np.asarray(inputs["L0_attn_Wk"], dtype=np.float32)
    wv = np.asarray(inputs["L0_attn_Wv"], dtype=np.float32)
    bv = np.asarray(inputs["L0_attn_bv"], dtype=np.float32)
    wo = np.asarray(inputs["L0_attn_Wo"], dtype=np.float32)
    wup = np.asarray(inputs["L0_ffn_W_up"], dtype=np.float32)
    gw = np.asarray(inputs["L0_ffn_gate_w"], dtype=np.float32)
    eb = np.asarray(inputs["L0_ffn_expert_bias"], dtype=np.float32)
    la = np.asarray(inputs["L0_ffn_lora_A"], dtype=np.float32)
    lb = np.asarray(inputs["L0_ffn_lora_B"], dtype=np.float32)

    c = np.ascontiguousarray
    wqT = c(wq.reshape(NQ, 128, NC, 128).transpose(0, 3, 2, 1))
    wkT = c(wk.reshape(NKV, 128, NC, 128).transpose(0, 3, 2, 1))
    wvR = c(wv.reshape(NKV * 128, NC, 128).transpose(1, 2, 0))
    woT = c(wo.reshape(NC, 128, NC, 128).transpose(0, 3, 2, 1))
    wupT = c(wup.reshape(NC, 128, NC, 128).transpose(0, 3, 2, 1))
    gateT = c(gw.T.reshape(NC, 128, 4))
    ebias = c(eb.reshape(4, 1))
    aT = c(la.reshape(32, D).T.reshape(NC, 128, 32))
    bflat = c(lb.transpose(0, 2, 1).reshape(32, D))
    bvc = c(bv.reshape(NKV, 128).T)
    xT = [c(x[b].T) for b in range(2)]

    in_maps = []
    for core in range(8):
        b, s = core // 4, core % 4
        in_maps.append({
            "xTb": xT[b],
            "xTmy": c(xT[b][:, MY * s:MY * (s + 1)]),
            "wqT": wqT, "wkT": wkT, "wvR": wvR, "woT": woT, "wupT": wupT,
            "gateT": gateT, "ebias": ebias, "aT": aT, "bflat": bflat,
            "bvc": bvc,
        })

    if "nc" not in _NC_CACHE:
        _NC_CACHE["nc"] = build_nc()
    nc = _NC_CACHE["nc"]
    res = run_bass_kernel_spmd(nc, in_maps, core_ids=list(range(8)))

    out = np.empty((2, S, D), np.float32)
    for core in range(8):
        b, s = core // 4, core % 4
        out[b, MY * s:MY * (s + 1), :] = res.results[core]["outT"].T
    return out


# version 26 (final)
# speedup vs baseline: 1.3081x; 1.1296x over previous
import numpy as np
import concourse.bass as bass
import concourse.mybir as mybir
from concourse.bass_utils import run_bass_kernel_spmd

F32 = mybir.dt.float32
BF16 = mybir.dt.bfloat16
AF = mybir.ActivationFunctionType
ALU = mybir.AluOpType
AX = mybir.AxisListType

D = 4096
S = 4096
MY = 1024
NQ, NKV, DH = 32, 8, 128
NC = D // 128
QSCALE = DH ** -0.5
EPS = 1e-5
P0T = 256
NT0 = S // P0T
PRT = 256
NTP = MY // PRT
SB = 512


class Item:
    __slots__ = ("meth", "args", "kwargs", "incs")

    def __init__(self, meth, args, kwargs):
        self.meth, self.args, self.kwargs = meth, args, kwargs
        self.incs = []

    def then_inc(self, sem, k):
        self.incs.append((sem, k))
        return self


class WaitOn:
    __slots__ = ("names",)

    def __init__(self, names):
        self.names = names


class Stream:
    def __init__(self):
        self.items = []

    def __getattr__(self, meth):
        if meth.startswith("__"):
            raise AttributeError(meth)

        def f(*a, **kw):
            it = Item(meth, a, kw)
            self.items.append(it)
            return it
        return f

    def wait(self, *names):
        if names:
            self.items.append(WaitOn(names))


class Ctr:
    def __init__(self, sem, step=1):
        self.sem, self.step, self.n = sem, step, 0

    def tick(self, item):
        item.then_inc(self.sem, self.step)
        self.n += self.step
        return self.n


def build_nc():
    nc = bass.Bass("TRN2", target_bir_lowering=False)

    xTb = nc.dram_tensor("xTb", [D, S], F32, kind="ExternalInput").ap()
    xTmy = nc.dram_tensor("xTmy", [D, MY], F32, kind="ExternalInput").ap()
    wqT = nc.dram_tensor("wqT", [NQ, 128, NC, 128], F32, kind="ExternalInput").ap()
    wkT = nc.dram_tensor("wkT", [NKV, 128, NC, 128], F32, kind="ExternalInput").ap()
    wvR = nc.dram_tensor("wvR", [NC, 128, NKV * 128], F32, kind="ExternalInput").ap()
    woT = nc.dram_tensor("woT", [NC, 128, NC, 128], F32, kind="ExternalInput").ap()
    wupT = nc.dram_tensor("wupT", [NC, 128, NC, 128], F32, kind="ExternalInput").ap()
    gateT = nc.dram_tensor("gateT", [NC, 128, 4], F32, kind="ExternalInput").ap()
    ebias = nc.dram_tensor("ebias", [4, 1], F32, kind="ExternalInput").ap()
    aT = nc.dram_tensor("aT", [NC, 128, 32], F32, kind="ExternalInput").ap()
    bflat = nc.dram_tensor("bflat", [32, D], F32, kind="ExternalInput").ap()
    bvc = nc.dram_tensor("bvc", [128, NKV], F32, kind="ExternalInput").ap()
    outT = nc.dram_tensor("outT", [D, MY], F32, kind="ExternalOutput").ap()

    k_d = nc.dram_tensor("k_d", [NKV, 128, S], BF16, kind="Internal").ap()
    xnst = nc.dram_tensor("xnst", [128, NC, MY], BF16, kind="Internal").ap()
    v_d = nc.dram_tensor("v_d", [NKV, S // 128, 128, 128], BF16, kind="Internal").ap()

    from contextlib import ExitStack
    with ExitStack() as top:
        ones_col = top.enter_context(nc.sbuf_tensor("ones_col", [128, 1], BF16))
        ones_row = top.enter_context(nc.sbuf_tensor("ones_row", [1, 128], F32))
        cst_eps = top.enter_context(nc.sbuf_tensor("cst_eps", [1, 1], F32))
        eb_sb = top.enter_context(nc.sbuf_tensor("eb_sb", [4, 1], F32))
        bvc_sb = top.enter_context(nc.sbuf_tensor("bvc_sb", [128, NKV], F32))
        idn = top.enter_context(nc.sbuf_tensor("idn", [128, 128], F32))
        e4 = top.enter_context(nc.sbuf_tensor("e4", [4, 32], BF16))
        gate_sb = top.enter_context(nc.sbuf_tensor("gate_sb", [128, NC, 4], BF16))
        m_row = top.enter_context(nc.sbuf_tensor("m_row", [1, P0T], F32))
        v_row = top.enter_context(nc.sbuf_tensor("v_row", [1, P0T], F32))
        r_row = top.enter_context(nc.sbuf_tensor("r_row", [1, P0T], F32))

        pb = [top.enter_context(nc.psum_tensor(f"pb{i}", [128, 512], F32))
              for i in range(8)]

        s0 = ExitStack()
        x0 = [s0.enter_context(nc.sbuf_tensor(f"x0_{i}", [128, NC, P0T], BF16))
              for i in range(2)]
        wk_sb = s0.enter_context(nc.sbuf_tensor("wk_sb", [128, NKV, NC, 128], BF16))
        wv_sb = s0.enter_context(nc.sbuf_tensor("wv_sb", [128, NC, NKV * 128], BF16))
        xn0 = [s0.enter_context(nc.sbuf_tensor(f"xn0_{i}", [128, NC, P0T], BF16))
               for i in range(2)]
        x20 = s0.enter_context(nc.sbuf_tensor("x20", [128, 2, P0T], BF16))
        xf0 = s0.enter_context(nc.sbuf_tensor("xf0", [128, 2, P0T], F32))
        kt0 = [s0.enter_context(nc.sbuf_tensor(f"kt0_{i}", [128, P0T], BF16))
               for i in range(2)]
        vt0 = [s0.enter_context(nc.sbuf_tensor(f"vt0_{i}", [128, NKV * 128], BF16))
               for i in range(2)]
        s0.close()

        xn_my = top.enter_context(nc.sbuf_tensor("xn_my", [128, NC, MY], BF16))
        ctx_sb = top.enter_context(nc.sbuf_tensor("ctx_sb", [128, NC, MY], BF16))

        s1 = ExitStack()
        wq_sb = [s1.enter_context(nc.sbuf_tensor(f"wq_sb{i}", [128, NC, 128], BF16))
                 for i in range(2)]
        kh_sb = [s1.enter_context(nc.sbuf_tensor(f"kh_sb{i}", [128, S], BF16))
                 for i in range(2)]
        vh_sb = [s1.enter_context(nc.sbuf_tensor(f"vh_sb{i}", [128, S // 128, 128], BF16))
                 for i in range(2)]
        q_sb = s1.enter_context(nc.sbuf_tensor("q_sb", [128, MY], BF16))
        expp = s1.enter_context(nc.sbuf_tensor("expp", [128, 4, SB], BF16))
        dpair = s1.enter_context(nc.sbuf_tensor("dpair", [128, 10, SB], BF16))
        recip_b = s1.enter_context(nc.sbuf_tensor("recip_b", [128, SB], F32))
        ct_f = s1.enter_context(nc.sbuf_tensor("ct_f", [128, SB], F32))
        den_row = s1.enter_context(nc.sbuf_tensor("den_row", [1, SB], F32))
        s1.close()

        s2 = ExitStack()
        wo_sb = [s2.enter_context(nc.sbuf_tensor(f"wo_sb{i}", [128, NC, 128], BF16))
                 for i in range(2)]
        wu_sb = [s2.enter_context(nc.sbuf_tensor(f"wu_sb{i}", [128, NC, 128], BF16))
                 for i in range(2)]
        xres = [s2.enter_context(nc.sbuf_tensor(f"xres{i}", [128, MY], F32))
                for i in range(2)]
        at_sb = s2.enter_context(nc.sbuf_tensor("at_sb", [128, NC, 32], BF16))
        bf_sb = s2.enter_context(nc.sbuf_tensor("bf_sb", [32, D], BF16))
        lg_sb = s2.enter_context(nc.sbuf_tensor("lg_sb", [4, MY], F32))
        lgT_sb = s2.enter_context(nc.sbuf_tensor("lgT_sb", [128, MY // 128, 4], F32))
        mx_sb = s2.enter_context(nc.sbuf_tensor("mx_sb", [128, MY // 128, 1], F32))
        oh_sb = s2.enter_context(nc.sbuf_tensor("oh_sb", [128, MY // 128, 4], F32))
        ohT_sb = s2.enter_context(nc.sbuf_tensor("ohT_sb", [4, MY], BF16))
        m32_sb = s2.enter_context(nc.sbuf_tensor("m32_sb", [32, MY], BF16))
        zsel_sb = s2.enter_context(nc.sbuf_tensor("zsel_sb", [32, MY], BF16))
        g_sb = s2.enter_context(nc.sbuf_tensor("g_sb", [128, SB], F32))
        t_sb = s2.enter_context(nc.sbuf_tensor("t_sb", [128, SB], F32))
        ot_sb = [s2.enter_context(nc.sbuf_tensor(f"ot_sb{i}", [128, MY], F32))
                 for i in range(2)]
        s2.close()

        sPE = Ctr(top.enter_context(nc.semaphore("sPE")))
        sDVE = Ctr(top.enter_context(nc.semaphore("sDVE")))
        sACT = Ctr(top.enter_context(nc.semaphore("sACT")))
        sPOOL = Ctr(top.enter_context(nc.semaphore("sPOOL")))
        def dsem(name):
            return Ctr(top.enter_context(nc.semaphore(name)), 16)

        qW0 = dsem("qW0")
        qX = dsem("qX")
        qK = [dsem("qK0"), dsem("qK1")]
        qV = [dsem("qV0"), dsem("qV1")]
        qQ = [dsem("qQ0"), dsem("qQ1")]
        qWo = [dsem("qWo0"), dsem("qWo1")]
        qWu = [dsem("qWu0"), dsem("qWu1")]
        qWb = dsem("qWb")
        qR = [dsem("qR0"), dsem("qR1")]
        qSK = [dsem("qSK0"), dsem("qSK1")]
        qSV = [dsem("qSV0"), dsem("qSV1")]
        qSO = [dsem("qSO0"), dsem("qSO1")]
        qXS = [dsem("qXS0"), dsem("qXS1")]
        qXM = dsem("qXM")
        qWK = [dsem(f"qWK{h}") for h in range(NKV)]
        qWV = [dsem("qWVa"), dsem("qWVb")]
        all_q = ([qW0, qX, qWb, qXM] + qK + qV + qQ + qWo + qWu + qR
                 + qSK + qSV + qSO + qXS + qWK + qWV)

        K = {}

        def mark(name, ctr, item):
            K[name] = (ctr.sem, ctr.tick(item))

        T, V, A, P, Y = Stream(), Stream(), Stream(), Stream(), Stream()

        sPOOL.tick(P.memset(ones_col[:], 1.0))
        sPOOL.tick(P.memset(ones_row[:], 1.0))
        sPOOL.tick(P.memset(cst_eps[:], EPS))
        mark("idn0", sPOOL, P.memset(idn[:], 0.0))
        P.wait("idn0")
        mark("idn", sPOOL,
             P.affine_select(out=idn[:], in_=idn[:], compare_op=ALU.not_equal,
                             fill=1.0, base=0, pattern=[[-1, 128]],
                             channel_multiplier=1))
        mark("e40", sPOOL, P.memset(e4[:], 1.0))
        P.wait("e40")
        mark("e41", sPOOL, P.affine_select(
            e4[:], e4[:], pattern=[[1, 32]], channel_multiplier=-8,
            base=0, compare_op=ALU.is_ge, fill=0.0))
        P.wait("e41")
        mark("consts", sPOOL, P.affine_select(
            e4[:], e4[:], pattern=[[-1, 32]], channel_multiplier=8,
            base=7, compare_op=ALU.is_ge, fill=0.0))
        K["e4done"] = K["consts"]
        qW0.tick(P.dma_start(eb_sb[:], ebias))
        qW0.tick(P.dma_start(bvc_sb[:], bvc))
        mark("gws", qW0, P.dma_start(gate_sb[:], gateT.rearrange("c p e -> p c e")))

        chunk_par = [0]

        def ln_tile(tag, xsrc, xn_dst, n_tok, first_waits, prev=None,
                    norm_first_waits=(), bcp=(2, 3)):
            ps_sum = pb[0][0:1, 0:n_tok]
            ps_sq = pb[1][0:1, 0:n_tok]
            ps_mb = pb[bcp[0]][:, 0:n_tok]
            ps_rb = pb[bcp[1]][:, 0:n_tok]
            for c in range(NC):
                j = chunk_par[0]
                chunk_par[0] += 1
                sl = j % 2
                V.wait(f"{tag}src" if c == 0 else None) if c == 0 else None
                if c == 0:
                    V.wait(f"{tag}src")
                if j >= 2:
                    V.wait(f"lnB_{j-2}")
                mark(f"lnq_{j}", sDVE,
                     V.tensor_mul(x20[:, sl, 0:n_tok], xsrc[:, c, 0:n_tok],
                                  xsrc[:, c, 0:n_tok]))
                T.wait(f"lnq_{j}")
                if c == 0:
                    T.wait(f"{tag}src", *first_waits)
                T.matmul(ps_sum, ones_col[:], xsrc[:, c, 0:n_tok],
                         start=(c == 0), stop=(c == NC - 1))
                i2 = T.matmul(ps_sq, ones_col[:], x20[:, sl, 0:n_tok],
                              start=(c == 0), stop=(c == NC - 1))
                mark(f"lnB_{j}", sPE, i2)
            jl = chunk_par[0] - 1
            V.wait(f"lnB_{jl}")
            if prev is not None:
                V.wait(f"{prev}bm", f"{prev}bc", f"{prev}ln")
            mark(f"{tag}m", sDVE,
                 V.tensor_scalar_mul(m_row[:, 0:n_tok], ps_sum, 1.0 / D))
            mark(f"{tag}v0", sDVE,
                 V.tensor_scalar_mul(v_row[:, 0:n_tok], ps_sq, 1.0 / D))
            V.wait(f"{tag}m", f"{tag}v0")
            mark(f"{tag}m2", sDVE,
                 V.tensor_mul(r_row[:, 0:n_tok], m_row[:, 0:n_tok],
                              m_row[:, 0:n_tok]))
            V.wait(f"{tag}m2")
            mark(f"{tag}var", sDVE,
                 V.tensor_sub(v_row[:, 0:n_tok], v_row[:, 0:n_tok],
                              r_row[:, 0:n_tok]))
            A.wait(f"{tag}var")
            if prev is not None:
                A.wait(f"{prev}bc")
            mark(f"{tag}ln", sACT,
                 A.activation(r_row[:, 0:n_tok], v_row[:, 0:n_tok], AF.Ln,
                              bias=cst_eps[:, :]))
            A.wait(f"{tag}ln")
            mark(f"{tag}rstd", sACT,
                 A.activation(r_row[:, 0:n_tok], r_row[:, 0:n_tok], AF.Exp,
                              scale=-0.5))
            T.wait(f"{tag}m")
            mark(f"{tag}bm", sPE,
                 T.matmul(ps_mb, ones_row[:], m_row[:, 0:n_tok],
                          start=True, stop=True))
            T.wait(f"{tag}rstd")
            mark(f"{tag}bc", sPE,
                 T.matmul(ps_rb, ones_row[:], r_row[:, 0:n_tok],
                          start=True, stop=True))
            for c in range(NC):
                sl = c % 2
                V.wait(f"{tag}bc")
                if c == 0:
                    V.wait(*norm_first_waits)
                mark(f"{tag}sub{c}", sDVE,
                     V.tensor_sub(xf0[:, sl, 0:n_tok], xsrc[:, c, 0:n_tok], ps_mb))
                V.wait(f"{tag}sub{c}")
                mark(f"{tag}xn{c}", sDVE,
                     V.tensor_tensor(xn_dst(c), xf0[:, sl, 0:n_tok], ps_rb,
                                     ALU.mult))


        def kv_emit(tt, xtag):
            for h in range(NKV):
                hh = tt * NKV + h
                if tt == 0:
                    T.wait(f"wkh_{h}")
                if hh > 0:
                    T.wait(f"kev_{hh-1}")
                if h == 0 and tt > 0:
                    T.wait(f"vev_{tt*2-1}_1")
                for c in range(NC):
                    if h == 0:
                        T.wait(f"{xtag}xn{c}")
                    i = T.matmul(pb[4][:, 0:P0T], wk_sb[:, h, c, :],
                                 xn0[tt % 2][:, c, :],
                                 start=(c == 0), stop=(c == NC - 1))
                mark(f"kmm_{hh}", sPE, i)
                A.wait(f"kmm_{hh}")
                if hh >= 2:
                    A.wait(f"kst_{hh-2}")
                mark(f"kev_{hh}", sACT,
                     A.activation(kt0[h % 2][:], pb[4][:, 0:P0T], AF.Copy))
                A.wait(f"kev_{hh}")
                mark(f"kst_{hh}", qSK[hh % 2],
                     A.dma_start(k_d[h, :, tt * P0T:(tt + 1) * P0T], kt0[h % 2][:]))
            for tc in range(P0T // 128):
                tcg = tt * (P0T // 128) + tc
                if tcg > 0:
                    T.wait(f"vev_{tcg-1}_0", f"vev_{tcg-1}_1")
                T.wait(f"kev_{tt*NKV+NKV-1}")
                for half in range(2):
                    ps = pb[5 if half == 0 else 4][:, 0:512]
                    if tt == 0 and tc == 0:
                        T.wait("wva" if half == 0 else "w0")
                    for c in range(NC):
                        i = T.matmul(ps, xn0[tt % 2][:, c, tc * 128:(tc + 1) * 128],
                                     wv_sb[:, c, half * 512:(half + 1) * 512],
                                     start=(c == 0), stop=(c == NC - 1))
                    mark(f"vmm_{tcg}_{half}", sPE, i)
                for half in range(2):
                    A.wait(f"vmm_{tcg}_{half}")
                    if tcg >= 2:
                        A.wait(f"vst_{tcg-2}")
                    mark(f"vev_{tcg}_{half}", sACT,
                         A.activation(vt0[tc % 2][:, half * 512:(half + 1) * 512],
                                      pb[5 if half == 0 else 4][:, 0:512], AF.Copy))
                A.wait(f"vev_{tcg}_0", f"vev_{tcg}_1")
                for h in range(NKV):
                    i = A.dma_start(v_d[h, tcg], vt0[tc % 2][:, h * 128:(h + 1) * 128])
                    if h < NKV - 1:
                        qSV[tcg % 2].tick(i)
                mark(f"vst_{tcg}", qSV[tcg % 2], i)

        seq = []
        bi = 0
        for i4 in range(4):
            seq += [("b", bi), ("b", bi + 1), ("b", bi + 2), ("b", bi + 3)]
            seq.append(("p", i4))
            bi += 4
        kvq = []
        done_kv = 0
        for sidx, (kind, ti) in enumerate(seq):
            tg = f"t{sidx}"
            if sidx >= 2:
                P.wait(f"t{sidx-2}xn{NC-1}", f"lnB_{(sidx-1)*NC-1}")
                if seq[sidx - 2][0] == "p":
                    P.wait(f"xst_{seq[sidx-2][1]}")
            if sidx >= 1:
                P.wait(f"lnq_{(sidx-1)*NC}")
            if kind == "b":
                src_ap = xTb.rearrange("(c p) t -> p c t", p=128)[
                    :, :, ti * P0T:(ti + 1) * P0T]
            else:
                src_ap = xTmy.rearrange("(c p) t -> p c t", p=128)[
                    :, :, ti * P0T:(ti + 1) * P0T]
            mark(f"{tg}src", qX, P.dma_start(x0[sidx % 2][:], src_ap))
            if sidx == 1:
                for h in range(2):
                    mark(f"wkh_{h}", qWK[h],
                         P.dma_start(wk_sb[:, h], wkT[h]))
            elif sidx == 2:
                mark("wva", qWV[0],
                     P.dma_start(wv_sb[:, :, 0:512],
                                 wvR.rearrange("c p v -> p c v")[:, :, 0:512]))
                for h in range(2, 4):
                    mark(f"wkh_{h}", qWK[h],
                         P.dma_start(wk_sb[:, h], wkT[h]))
            elif sidx == 3:
                mark("w0", qWV[1],
                     P.dma_start(wv_sb[:, :, 512:1024],
                                 wvR.rearrange("c p v -> p c v")[:, :, 512:1024]))
                for h in range(4, NKV):
                    mark(f"wkh_{h}", qWK[h],
                         P.dma_start(wk_sb[:, h], wkT[h]))
            fw = ["consts"] if sidx == 0 else [f"t{sidx-1}m", f"t{sidx-1}v0"]
            if sidx >= 2:
                fw.append(f"t{sidx-2}xn{NC-1}")
            if kind == "b":
                nfw = [f"vmm_{(ti-2)*2+1}_1"] if ti >= 2 else ()
                xnd = (lambda c, sl=ti % 2: xn0[sl][:, c, 0:P0T])
            else:
                nfw = ()
                xnd = (lambda c, sl=sidx % 2: x0[sl][:, c, 0:P0T])
            ln_tile(tg, x0[sidx % 2], xnd, P0T, fw,
                    prev=(f"t{sidx-1}" if sidx > 0 else None),
                    norm_first_waits=nfw,
                    bcp=((2, 3) if sidx % 2 == 0 else (6, 7)))
            if kind == "b":
                kvq.append((ti, tg, sidx))
            else:
                Y.wait(f"{tg}xn{NC-1}")
                mark(f"xst_{ti}", qXS[ti % 2],
                     Y.dma_start(xnst[:, :, ti * P0T:(ti + 1) * P0T],
                                 x0[sidx % 2][:]))
            if kvq and kvq[0][2] < sidx:
                bti, btg, _ = kvq.pop(0)
                kv_emit(bti, btg)
        while kvq:
            bti, btg, _ = kvq.pop(0)
            kv_emit(bti, btg)

        k_done = (f"kst_{NT0*NKV-2}", f"kst_{NT0*NKV-1}")
        v_done = (f"vst_{NT0*(P0T//128)-2}", f"vst_{NT0*(P0T//128)-1}")
        p0_pe_done = f"vmm_{NT0*(P0T//128)-1}_1"

        Y.wait("xst_0", "xst_1", "xst_2", "xst_3",
               f"t{len(seq)-1}xn{NC-1}", f"lnB_{len(seq)*NC-1}",
               *k_done, *v_done, p0_pe_done)
        mark("xnmy", qXM, Y.dma_start(xn_my[:], xnst))

        pro_done = "xnmy"
        pro_rows = [f"t{NT0+NTP-1}m", f"t{NT0+NTP-1}v0"]

        for h in range(NKV):
            P.wait(*k_done, *v_done)
            if h >= 2:
                P.wait(f"qk_{4*(h-2)+3}_1_31")
            mark(f"kh_{h}", qK[h % 2], P.dma_start(kh_sb[h % 2][:], k_d[h]))
            if h >= 2:
                P.wait(f"pvl_{4*(h-2)+3}_1")
            mark(f"vh_{h}", qV[h % 2],
                 P.dma_start(vh_sb[h % 2][:],
                             v_d[h].rearrange("c p d -> p c d")))
            for g in range(4):
                qh = h * 4 + g
                if qh < 2:
                    P.wait(pro_done, *pro_rows, *k_done, *v_done, p0_pe_done)
                else:
                    P.wait(f"qmm_{qh-2}_1")
                mark(f"wq_{qh}", qQ[qh % 2], P.dma_start(wq_sb[qh % 2][:], wqT[qh]))

        gchunk = [0]

        def qproj_emit(qh, interleave=None):
            if interleave is None:
                T.wait(f"wq_{qh}")
                if qh == 0:
                    T.wait(pro_done, *pro_rows)
                for sb2 in range(2):
                    for c in range(NC):
                        i = T.matmul(pb[sb2][:, 0:512], wq_sb[qh % 2][:, c, :],
                                     xn_my[:, c, sb2 * 512:(sb2 + 1) * 512],
                                     start=(c == 0), stop=(c == NC - 1))
                    mark(f"qmm_{qh}_{sb2}", sPE, i)
                return
            it = interleave
            sb2, c = it // NC, it % NC
            if it == 0:
                T.wait(f"wq_{qh}")
                T.wait(f"qev_{qh-1}_0", f"qev_{qh-1}_1")
            i = T.matmul(pb[sb2][:, 0:512], wq_sb[qh % 2][:, c, :],
                         xn_my[:, c, sb2 * 512:(sb2 + 1) * 512],
                         start=(c == 0), stop=(c == NC - 1))
            if c == NC - 1:
                mark(f"qmm_{qh}_{sb2}", sPE, i)

        def qev_emit(qh):
            for sb2 in range(2):
                V.wait(f"qmm_{qh}_{sb2}")
                if qh > 0:
                    V.wait(f"qk_{qh-1}_1_31")
                mark(f"qev_{qh}_{sb2}", sDVE,
                     V.tensor_scalar(q_sb[:, sb2 * 512:(sb2 + 1) * 512],
                                     pb[sb2][:, 0:512], QSCALE, None,
                                     op0=ALU.mult))

        qproj_emit(0)
        qev_emit(0)
        pending_bc = [None]
        pending_vt = [None]
        for kvh in range(NKV):
            for g in range(4):
                qh = kvh * 4 + g
                for sb in range(2):
                    jbase = gchunk[0]
                    T.wait(f"qev_{qh}_{sb}", f"kh_{kvh}")
                    SCB = (2, 3, 7)
                    for c in range(3):
                        j = jbase + c
                        if j >= 3:
                            T.wait(f"exp_{j-3}")
                        mark(f"qk_{qh}_{sb}_{c}", sPE,
                             T.matmul(pb[SCB[j % 3]][:, 0:SB],
                                      kh_sb[kvh % 2][:, c * 128:(c + 1) * 128],
                                      q_sb[:, sb * 512:(sb + 1) * 512],
                                      start=True, stop=True))
                    for c in range(NC):
                        j = jbase + c
                        A.wait(f"qk_{qh}_{sb}_{c}")
                        if j >= 4:
                            A.wait(f"pr_{(j-4)//2}")
                        mark(f"exp_{j}", sACT,
                             A.activation(expp[:, j % 4, :], pb[SCB[j % 3]][:, 0:SB],
                                          AF.Exp))
                        T.wait(f"exp_{j}")
                        if c == 0:
                            T.wait(f"vh_{kvh}")
                            if qh > 0 or sb > 0:
                                pq, psb = (qh, 0) if sb else (qh - 1, 1)
                                T.wait(f"ctxm_{pq}_{psb}")
                        i = T.matmul(pb[4][:, 0:SB], vh_sb[kvh % 2][:, c, :],
                                     expp[:, j % 4, :],
                                     start=(c == 0), stop=(c == NC - 1))
                        if c == NC - 1:
                            mark(f"pvl_{qh}_{sb}", sPE, i)
                        if qh < NQ - 1:
                            qproj_emit(qh + 1, interleave=sb * NC + c)
                        if c % 2 == 1:
                            V.wait(f"exp_{j}")
                            if qh == 0 and sb == 0 and c == 1:
                                V.wait(f"vev_{NT0*(P0T//128)-1}_1")
                            if j // 2 >= 2:
                                V.wait(f"qd_{(j-4)//4}")
                            mark(f"pr_{j//2}", sDVE,
                                 V.tensor_add(dpair[:, 8 + ((c - 1) // 2) % 2, :],
                                              expp[:, (j - 1) % 4, :],
                                              expp[:, j % 4, :]))
                        if c % 4 == 3:
                            V.wait(f"pr_{j//2}", f"pr_{(j-2)//2}")
                            if c == 3 and (qh > 0 or sb > 0):
                                V.wait(f"den_{jbase-1}")
                            mark(f"qd_{j//4}", sDVE,
                                 V.tensor_add(dpair[:, (c - 3) // 4, :],
                                              dpair[:, 8, :],
                                              dpair[:, 9, :]))
                        if c == 5 and pending_vt[0] is not None:
                            V.wait("gws")
                            pq, psb, pkv = pending_vt[0]
                            V.wait(f"bc_{pq}_{psb}")
                            mark(f"rc_{pq}_{psb}", sDVE,
                                 V.tensor_copy(recip_b[:], pb[6][:, 0:SB]))
                            V.wait(f"rc_{pq}_{psb}")
                            mark(f"ctxn_{pq}_{psb}", sDVE,
                                 V.tensor_tensor(ct_f[:], ct_f[:], recip_b[:],
                                                 ALU.mult))
                            V.wait(f"ctxn_{pq}_{psb}")
                            mark(f"ctxw_{pq}_{psb}", sDVE,
                                 V.tensor_scalar(
                                     ctx_sb[:, pq, psb * 512:(psb + 1) * 512],
                                     ct_f[:], bvc_sb[:, pkv:pkv + 1], None,
                                     op0=ALU.add))
                            pending_vt[0] = None
                        if c == 3 and pending_bc[0] is not None:
                            pq, psb = pending_bc[0]
                            T.wait(f"dre_{pq}_{psb}")
                            if pq > 0 or psb > 0:
                                p2q, p2sb = (pq, 0) if psb else (pq - 1, 1)
                                T.wait(f"rc_{p2q}_{p2sb}")
                            mark(f"bc_{pq}_{psb}", sPE,
                                 T.matmul(pb[6][:, 0:SB], ones_row[:], den_row[:],
                                          start=True, stop=True))
                            pending_bc[0] = None
                        if c + 3 < NC:
                            j2 = jbase + c + 3
                            T.wait(f"exp_{j2-3}")
                            mark(f"qk_{qh}_{sb}_{c+3}", sPE,
                                 T.matmul(pb[SCB[j2 % 3]][:, 0:SB],
                                          kh_sb[kvh % 2][:, (c + 2 + 1) * 128:(c + 4) * 128],
                                          q_sb[:, sb * 512:(sb + 1) * 512],
                                          start=True, stop=True))
                    if qh > 0 or sb > 0:
                        pq, psb = (qh, 0) if sb else (qh - 1, 1)
                        T.wait(f"dln_{pq}_{psb}")
                    for i2 in range(NC // 4):
                        T.wait(f"qd_{(jbase + 4*i2 + 3)//4}")
                        di = T.matmul(pb[5][0:1, 0:SB], ones_col[:],
                                      dpair[:, i2, :],
                                      start=(i2 == 0), stop=(i2 == NC // 4 - 1))
                    mark(f"den_{jbase+NC-1}", sPE, di)
                    gchunk[0] += NC
                    V.wait(f"pvl_{qh}_{sb}")
                    if qh > 0 or sb > 0:
                        pq, psb = (qh, 0) if sb else (qh - 1, 1)
                        V.wait(f"ctxw_{pq}_{psb}")
                    mark(f"ctxm_{qh}_{sb}", sDVE,
                         V.tensor_copy(ct_f[:], pb[4][:, 0:SB]))
                    A.wait(f"den_{jbase+NC-1}")
                    if qh > 0 or sb > 0:
                        pq, psb = (qh, 0) if sb else (qh - 1, 1)
                        A.wait(f"bc_{pq}_{psb}")
                        A.wait(f"dre_{pq}_{psb}")
                    mark(f"dln_{qh}_{sb}", sACT,
                         A.activation(den_row[:], pb[5][0:1, 0:SB], AF.Ln))
                    A.wait(f"dln_{qh}_{sb}")
                    mark(f"dre_{qh}_{sb}", sACT,
                         A.activation(den_row[:], den_row[:], AF.Exp, scale=-1.0))
                    pending_bc[0] = (qh, sb)
                    pending_vt[0] = (qh, sb, kvh)
                if qh < NQ - 1:
                    qev_emit(qh + 1)

        pq, psb = pending_bc[0]
        T.wait(f"dre_{pq}_{psb}")
        p2q, p2sb = (pq, 0) if psb else (pq - 1, 1)
        T.wait(f"rc_{p2q}_{p2sb}")
        mark(f"bc_{pq}_{psb}", sPE,
             T.matmul(pb[6][:, 0:SB], ones_row[:], den_row[:],
                      start=True, stop=True))
        pending_bc[0] = None
        pq, psb, pkv = pending_vt[0]
        V.wait("gws")
        V.wait(f"bc_{pq}_{psb}")
        mark(f"rc_{pq}_{psb}", sDVE, V.tensor_copy(recip_b[:], pb[6][:, 0:SB]))
        V.wait(f"rc_{pq}_{psb}")
        mark(f"ctxn_{pq}_{psb}", sDVE,
             V.tensor_tensor(ct_f[:], ct_f[:], recip_b[:], ALU.mult))
        V.wait(f"ctxn_{pq}_{psb}")
        mark(f"ctxw_{pq}_{psb}", sDVE,
             V.tensor_scalar(ctx_sb[:, pq, psb * 512:(psb + 1) * 512],
                             ct_f[:], bvc_sb[:, pkv:pkv + 1], None, op0=ALU.add))
        pending_vt[0] = None

        att_done = f"ctxw_{NQ-1}_1"
        att_pe_done = f"den_{gchunk[0]-1}"

        P.wait(att_done, att_pe_done)
        qWb.tick(P.dma_start(at_sb[:], aT.rearrange("c p e -> p c e")))
        mark("w1b", qWb, P.dma_start(bf_sb[:], bflat))
        for oc in range(NC):
            if oc >= 2:
                P.wait(f"womm_{oc-2}_1")
            mark(f"wo_{oc}", qWo[oc % 2], P.dma_start(wo_sb[oc % 2][:], woT[oc]))
            if oc >= 2:
                P.wait(f"ffnm_{oc-2}_1")
            mark(f"wu_{oc}", qWu[oc % 2], P.dma_start(wu_sb[oc % 2][:], wupT[oc]))
            if oc < 2:
                Y.wait(att_done)
            else:
                Y.wait(f"ow_{oc-2}_1")
            mark(f"xr_{oc}", qR[oc % 2],
                 Y.dma_start(xres[oc % 2][:], xTmy[oc * 128:(oc + 1) * 128, :]))

        def womm_emit(oc):
            T.wait(f"wo_{oc}")
            if oc >= 2:
                T.wait(f"tm_{oc-2}_0", f"tm_{oc-2}_1")
            pp = 4 if oc % 2 == 0 else 6
            for sb in range(2):
                for c in range(NC):
                    i = T.matmul(pb[pp + sb][:, 0:512], wo_sb[oc % 2][:, c, :],
                                 ctx_sb[:, c, sb * 512:(sb + 1) * 512],
                                 start=(c == 0), stop=(c == NC - 1))
                mark(f"womm_{oc}_{sb}", sPE, i)

        def ffnm_emit(oc):
            T.wait(f"wu_{oc}", "zs_0", "zs_1")
            if oc >= 2:
                T.wait(f"gev_{oc-2}_1")
            elif oc == 0:
                T.wait("lgev_0", "lgev_1")
            qp = 0 if oc % 2 == 0 else 2
            for sb in range(2):
                for c in range(NC):
                    T.matmul(pb[qp + sb][:, 0:512], wu_sb[oc % 2][:, c, :],
                             xn_my[:, c, sb * 512:(sb + 1) * 512],
                             start=(c == 0), stop=False)
                mark(f"ffnm_{oc}_{sb}", sPE,
                     T.matmul(pb[qp + sb][:, 0:512],
                              bf_sb[:, oc * 128:(oc + 1) * 128],
                              zsel_sb[:, sb * 512:(sb + 1) * 512],
                              start=False, stop=True))

        def tail_emit(oc):
            pp = 4 if oc % 2 == 0 else 6
            qp = 0 if oc % 2 == 0 else 2
            for sb in range(2):
                A.wait(f"ffnm_{oc}_{sb}")
                if oc > 0 or sb > 0:
                    poc, psb = (oc, 0) if sb else (oc - 1, 1)
                    A.wait(f"tm_{poc}_{psb}")
                mark(f"gev_{oc}_{sb}", sACT,
                     A.activation(g_sb[:], pb[qp + sb][:, 0:512], AF.Silu))
                V.wait(f"womm_{oc}_{sb}", f"gev_{oc}_{sb}")
                if oc > 0 or sb > 0:
                    poc, psb = (oc, 0) if sb else (oc - 1, 1)
                    V.wait(f"ow_{poc}_{psb}")
                mark(f"tm_{oc}_{sb}", sDVE,
                     V.tensor_tensor(t_sb[:], pb[pp + sb][:, 0:512],
                                     g_sb[:], ALU.mult))
                V.wait(f"tm_{oc}_{sb}", f"xr_{oc}")
                if oc >= 2:
                    V.wait(f"ost_{oc-2}")
                mark(f"ow_{oc}_{sb}", sDVE,
                     V.tensor_add(ot_sb[oc % 2][:, sb * 512:(sb + 1) * 512],
                                  t_sb[:],
                                  xres[oc % 2][:, sb * 512:(sb + 1) * 512]))
            A.wait(f"ow_{oc}_0", f"ow_{oc}_1")
            mark(f"ost_{oc}", qSO[oc % 2],
                 A.dma_start(outT[oc * 128:(oc + 1) * 128, :], ot_sb[oc % 2][:]))

        T.wait(att_done, att_pe_done, "gws")
        for sb in range(2):
            T.wait(f"qev_{NQ-1}_{sb}")
            for c in range(NC):
                i = T.matmul(pb[sb][0:4, 0:512], gate_sb[:, c, :],
                             xn_my[:, c, sb * 512:(sb + 1) * 512],
                             start=(c == 0), stop=(c == NC - 1))
            mark(f"lgmm_{sb}", sPE, i)
            V.wait(f"lgmm_{sb}", att_done, "gws")
            mark(f"lgev_{sb}", sDVE,
                 V.tensor_scalar(lg_sb[:, sb * 512:(sb + 1) * 512],
                                 pb[sb][0:4, 0:512], eb_sb[:, 0:1], None,
                                 op0=ALU.add))
        womm_emit(0)
        womm_emit(1)
        for b in range(MY // 128):
            T.wait(f"lgev_{b // 4}", "idn")
            if b > 0:
                T.wait(f"lgtc_{b-1}")
            mark(f"lgt_{b}", sPE,
                 T.transpose(pb[2][:, 0:4], lg_sb[:, b * 128:(b + 1) * 128],
                             idn[0:4, 0:4]))
            V.wait(f"lgt_{b}")
            mark(f"lgtc_{b}", sDVE, V.tensor_copy(lgT_sb[:, b, :], pb[2][:, 0:4]))
        for b in range(MY // 128):
            V.wait(f"lgtc_{b}")
            mark(f"mx_{b}", sDVE,
                 V.reduce_max(mx_sb[:, b, :], lgT_sb[:, b, :], axis=AX.X))
            V.wait(f"mx_{b}")
            mark(f"oh_{b}", sDVE,
                 V.tensor_scalar(oh_sb[:, b, :], lgT_sb[:, b, :], mx_sb[:, b, 0:1],
                                 None, op0=ALU.is_ge))
        for b in range(MY // 128):
            T.wait(f"oh_{b}", f"lgtc_{MY//128-1}")
            if b > 0:
                T.wait(f"ohtc_{b-1}")
            mark(f"oht_{b}", sPE,
                 T.transpose(pb[2][0:4, 0:128], oh_sb[:, b, :], idn[:, :]))
            V.wait(f"oht_{b}")
            mark(f"ohtc_{b}", sDVE,
                 V.tensor_copy(ohT_sb[:, b * 128:(b + 1) * 128], pb[2][0:4, 0:128]))
        for sb in range(2):
            T.wait(f"ohtc_{MY//128-1}", "e4done")
            if sb > 0:
                T.wait("m32e_0")
            mark(f"m32m_{sb}", sPE,
                 T.matmul(pb[2][0:32, 0:512], e4[:],
                          ohT_sb[:, sb * 512:(sb + 1) * 512], start=True, stop=True))
            V.wait(f"m32m_{sb}")
            mark(f"m32e_{sb}", sDVE,
                 V.tensor_copy(m32_sb[:, sb * 512:(sb + 1) * 512],
                               pb[2][0:32, 0:512]))
        for sb in range(2):
            if sb > 0:
                T.wait("zs_0")
            else:
                T.wait("w1b")
            for c in range(NC):
                i = T.matmul(pb[3][0:32, 0:512], at_sb[:, c, :],
                             xn_my[:, c, sb * 512:(sb + 1) * 512],
                             start=(c == 0), stop=(c == NC - 1))
            mark(f"zmm_{sb}", sPE, i)
            V.wait(f"zmm_{sb}", f"m32e_{sb}")
            mark(f"zs_{sb}", sDVE,
                 V.tensor_tensor(zsel_sb[:, sb * 512:(sb + 1) * 512],
                                 pb[3][0:32, 0:512],
                                 m32_sb[:, sb * 512:(sb + 1) * 512], ALU.mult))
        ffnm_emit(0)
        tail_emit(0)
        ffnm_emit(1)
        tail_emit(1)
        for oc in range(2, NC):
            womm_emit(oc)
            ffnm_emit(oc)
            tail_emit(oc)

        Y.wait(f"ost_{NC-2}", f"ost_{NC-1}")
        Y.items.append(WaitOn(tuple(f"__q{i}__" for i in range(len(all_q)))))

        for i, q in enumerate(all_q):
            K[f"__q{i}__"] = (q.sem, q.n)

        def replay(items):
            def f(eng):
                seen = {}
                for it in items:
                    if isinstance(it, WaitOn):
                        for nm in it.names:
                            if nm is None:
                                continue
                            sem, n = K[nm]
                            if seen.get(sem.num, -1) >= n:
                                continue
                            seen[sem.num] = n
                            eng.wait_ge(sem, n)
                    else:
                        inst = getattr(eng, it.meth)(*it.args, **it.kwargs)
                        for sem, k in it.incs:
                            inst.then_inc(sem, k)
            return f

        with nc.Block() as block:
            block.tensor(replay(T.items))
            block.vector(replay(V.items))
            block.scalar(replay(A.items))
            block.gpsimd(replay(P.items))
            block.sync(replay(Y.items))

    return nc


_NC_CACHE = {}


def kernel(**inputs):
    x = np.asarray(inputs["L0_attn_hidden_states"], dtype=np.float32)
    wq = np.asarray(inputs["L0_attn_Wq"], dtype=np.float32)
    wk = np.asarray(inputs["L0_attn_Wk"], dtype=np.float32)
    wv = np.asarray(inputs["L0_attn_Wv"], dtype=np.float32)
    bv = np.asarray(inputs["L0_attn_bv"], dtype=np.float32)
    wo = np.asarray(inputs["L0_attn_Wo"], dtype=np.float32)
    wup = np.asarray(inputs["L0_ffn_W_up"], dtype=np.float32)
    gw = np.asarray(inputs["L0_ffn_gate_w"], dtype=np.float32)
    eb = np.asarray(inputs["L0_ffn_expert_bias"], dtype=np.float32)
    la = np.asarray(inputs["L0_ffn_lora_A"], dtype=np.float32)
    lb = np.asarray(inputs["L0_ffn_lora_B"], dtype=np.float32)

    c = np.ascontiguousarray
    wqT = c(wq.reshape(NQ, 128, NC, 128).transpose(0, 3, 2, 1))
    wkT = c(wk.reshape(NKV, 128, NC, 128).transpose(0, 3, 2, 1))
    wvR = c(wv.reshape(NKV * 128, NC, 128).transpose(1, 2, 0))
    woT = c(wo.reshape(NC, 128, NC, 128).transpose(0, 3, 2, 1))
    wupT = c(wup.reshape(NC, 128, NC, 128).transpose(0, 3, 2, 1))
    gateT = c(gw.T.reshape(NC, 128, 4))
    ebias = c(eb.reshape(4, 1))
    aT = c(la.reshape(32, D).T.reshape(NC, 128, 32))
    bflat = c(lb.transpose(0, 2, 1).reshape(32, D))
    bvc = c(bv.reshape(NKV, 128).T)
    xT = [c(x[b].T) for b in range(2)]

    in_maps = []
    for core in range(8):
        b, s = core // 4, core % 4
        in_maps.append({
            "xTb": xT[b],
            "xTmy": c(xT[b][:, MY * s:MY * (s + 1)]),
            "wqT": wqT, "wkT": wkT, "wvR": wvR, "woT": woT, "wupT": wupT,
            "gateT": gateT, "ebias": ebias, "aT": aT, "bflat": bflat,
            "bvc": bvc,
        })

    if "nc" not in _NC_CACHE:
        _NC_CACHE["nc"] = build_nc()
    nc = _NC_CACHE["nc"]
    res = run_bass_kernel_spmd(nc, in_maps, core_ids=list(range(8)))

    out = np.empty((2, S, D), np.float32)
    for core in range(8):
        b, s = core // 4, core % 4
        out[b, MY * s:MY * (s + 1), :] = res.results[core]["outT"].T
    return out
